# version 10
# baseline (speedup 1.0000x reference)
import contextlib
import numpy as np
import concourse.bass as bass
import concourse.mybir as mybir
from concourse.bass_utils import run_bass_kernel_spmd

F32 = mybir.dt.float32
BF16 = mybir.dt.bfloat16
I8 = mybir.dt.int8
AF = mybir.ActivationFunctionType
ALU = mybir.AluOpType

D = 2048
NT = 16
RK = 512
ROWS = 1024
C_Q = float(128 ** 3)
C_KV = 128.0
EPS = 1e-5

_EK = ("t", "v", "s", "g", "q")


class Bld:
    def __init__(self, nc, block, semnames, es):
        self.nc = nc
        self.sec = {"t": block.tensor, "v": block.vector, "s": block.scalar,
                    "g": block.gpsimd, "q": block.sync}
        self.sem, self.cnt = {}, {}
        self.seen = {k: {} for k in _EK}
        for n in semnames:
            self.sem[n] = es.enter_context(nc.semaphore(n))
            self.cnt[n] = 0
        self.wr, self.rd = {}, {}
        self.pend = {}

    def emit(self, ekey, deps, fn, inc):
        step = 16 if inc.startswith("D") else 1
        todo, seen = [], self.seen[ekey]
        for d in deps:
            if d is None:
                continue
            s, v = d
            if v > 0 and seen.get(s, 0) < v:
                todo.append((s, v))
                seen[s] = v
        semh, inc_sem = self.sem, self.sem[inc]

        def body(eng):
            for s, v in todo:
                eng.wait_ge(semh[s], v)
            fn(eng).then_inc(inc_sem, step)

        self.pend.setdefault(ekey, []).append(body)
        self.cnt[inc] += step
        return (inc, self.cnt[inc])

    def flush(self):
        for ekey, bodies in self.pend.items():
            def run_all(eng, bodies_=bodies):
                for b in bodies_:
                    b(eng)
            self.sec[ekey](run_all)
        self.pend = {}

    def dep_r(self, *bufs):
        return [self.wr.get(b) for b in bufs]

    def dep_w(self, *bufs):
        out = []
        for b in bufs:
            out.extend(self.rd.get(b, {}).items())
            if b in self.wr:
                out.append(self.wr[b])
        return out

    def did_r(self, mark, *bufs):
        for b in bufs:
            d = self.rd.setdefault(b, {})
            if d.get(mark[0], 0) < mark[1]:
                d[mark[0]] = mark[1]

    def did_w(self, mark, *bufs):
        for b in bufs:
            self.wr[b] = mark
            self.rd.pop(b, None)


def build_program():
    nc = bass.Bass("TRN2", target_bir_lowering=False)
    dti = lambda n, s, d: nc.dram_tensor(n, s, d, kind="ExternalInput").ap()
    x = dti("x", [D, ROWS], F32)
    wq = dti("wq", [16, 128, 16, 128], F32)
    wkv = dti("wkv", [16, 128, 16, 128], F32)
    wo = dti("wo", [16, 128, 16, 128], F32)
    wg = dti("wg", [64, 128, 16, 128], I8)
    wu = dti("wu", [64, 128, 16, 128], I8)
    wd = dti("wd", [16, 128, 64, 128], I8)
    sg = dti("sg", [128, 64], F32)
    su = dti("su", [128, 64], F32)
    sd = dti("sd", [128, 16], F32)
    cfm = dti("cf", [128, 128], F32)
    sfm = dti("sf", [128, 128], F32)
    mc = dti("mc", [128, 8], F32)
    out = nc.dram_tensor("out", [D, ROWS], F32, kind="ExternalOutput").ap()
    ascr = nc.dram_tensor("ascr", [D, ROWS], BF16, kind="Internal").ap()
    cc_in = nc.dram_tensor("cc_in", [128, 16], F32, kind="Internal")
    cc_out = nc.dram_tensor("cc_out", [8 * 128, 16], F32, kind="Internal")

    sems = ["T", "V", "S", "G", "CC", "DC",
            "DX0", "DX1", "DW0", "DW1", "DG0", "DG1", "DU0", "DU1",
            "DD0", "DD1", "DA0", "DA1", "DB0", "DB1",
            "DI", "DH", "DO"]

    es = contextlib.ExitStack()
    with es:
        sb = lambda n, s, d: es.enter_context(nc.sbuf_tensor(n, s, d))
        xnb = [sb("xn0", [128, NT * RK], BF16), sb("xn1", [128, NT * RK], BF16)]
        kvb = [sb("kv0", [128, NT * RK], BF16), sb("kv1", [128, NT * RK], BF16)]
        qr = sb("qr", [128, 4 * RK], BF16)
        sqr = sb("sqr", [128, 2 * RK], BF16)
        rcp = sb("rcp", [128, 2 * RK], F32)
        mbuf = [sb("mb0", [128, RK], F32), sb("mb1", [128, RK], F32)]
        rsdb = [sb("rsd0", [128, RK], F32), sb("rsd1", [128, RK], F32)]
        wdb = [sb("wdb0", [128, 2048], BF16), sb("wdb1", [128, 2048], BF16)]
        hb = [[sb(f"hb{p}_{i}", [128, RK], BF16) for i in range(6)] for p in range(2)]
        wbuf = [sb("wb%d" % i, [128, 2048], BF16) for i in range(4)]
        wgb = [sb("wgb0", [128, 2048], BF16), sb("wgb1", [128, 2048], BF16)]
        wub = [sb("wub0", [128, 2048], BF16), sb("wub1", [128, 2048], BF16)]
        fuse = sb("fuse", [128, 16 * RK], BF16)
        silg = [sb("silg0", [128, RK], BF16), sb("silg1", [128, RK], BF16)]
        acc = sb("acc", [128, NT * RK], F32)
        ast = [sb("ast0", [128, RK], BF16), sb("ast1", [128, RK], BF16)]
        a_rd = [sb("ard0", [128, RK], BF16), sb("ard1", [128, RK], BF16)]
        x_rd = [sb("xrd0", [128, RK], F32), sb("xrd1", [128, RK], F32)]
        axr = [sb("axr0", [128, RK], F32), sb("axr1", [128, RK], F32)]
        outst = [sb("ost0", [128, RK], F32), sb("ost1", [128, RK], F32)]
        tots = [sb("tot0", [128, 16], F32), sb("tot1", [128, 16], F32)]
        ctot = sb("ctot", [128, 16], F32)
        carry = [sb("carA", [128, 16], F32), sb("car1", [128, 16], F32)]
        gth = sb("gth", [128, 128], F32)
        sgv = sb("sgv", [128, 64], F32)
        suv = sb("suv", [128, 64], F32)
        sdv = sb("sdv", [128, 16], F32)
        cfb = sb("cfb", [128, 128], BF16)
        sfb = sb("sfb", [128, 128], BF16)
        mcb = sb("mcb", [128, 8], F32)
        onesF = sb("onesF", [128, 128], BF16)
        onesS = sb("onesS", [128, RK], BF16)
        P = [es.enter_context(nc.psum_tensor(f"P{i}", [128, 512], F32)) for i in range(8)]
        print('SBUF remaining after alloc:', nc.sbuf_bytes_remaining)
        block = es.enter_context(nc.Block())
        B = Bld(nc, block, sems, es)


        def prolog(g):
            g.memset(onesF[:], 1.0)
            g.memset(onesS[:], 1.0)
            return g.memset(carry[0][:], 0.0)
        mk = B.emit("g", [], prolog, "G")
        B.did_w(mk, "onesF", "onesS", "carA")
        for dst, src_, nm in ((sgv, sg, "sgv"), (suv, su, "suv"), (sdv, sd, "sdv"),
                              (cfb, cfm, "cfb"), (sfb, sfm, "sfb"), (mcb, mc, "mcb")):
            mk = B.emit("g", [], (lambda d_, s_: (lambda g: g.dma_start(out=d_[:], in_=s_)))(dst, src_), "DC")
            B.did_w(mk, nm)
        for c in (0, 1):
            mk = B.emit("g", [],
                        lambda g, c_=c: g.dma_start(
                            out=xnb[c_][:, 0:NT * RK].rearrange("p (t r) -> p t r", t=NT),
                            in_=x[:, c_ * RK:(c_ + 1) * RK].rearrange("(t p) r -> p t r", p=128)),
                        "DX%d" % c)
            B.did_w(mk, *["xn%d_t%d" % (c, t) for t in range(NT)])

        XN = [["xn%d_t%d" % (c, t) for t in range(NT)] for c in (0, 1)]

        def dma_wtile(wsrc, mt, bufs, bname, sem_base, nring=2):
            sl = mt % nring
            nm = bname + str(sl)
            mk = B.emit("g", B.dep_w(nm),
                        lambda g, w_=wsrc, m_=mt, d_=bufs[sl]: g.dma_start(
                            out=d_[:], in_=w_[m_].rearrange("p a b -> p (a b)")),
                        "%s%d" % (sem_base, sl % 2))
            B.did_w(mk, nm)

        def dma_wd_piece(idx):
            ot, grp, sl = idx % 16, idx // 16, idx % 2
            nm = "wdb%d" % sl
            mk = B.emit("g", B.dep_w(nm),
                        lambda g, o_=ot, p_=grp, d_=wdb[sl]: g.dma_start(
                            out=d_[:], in_=wd[o_, :, p_ * 16:(p_ + 1) * 16, :].rearrange("p a b -> p (a b)")),
                        "DD%d" % sl)
            B.did_w(mk, nm)

        def ln_chunk(c):
            xname = XN[c]
            ps, pq = 6, 7
            psn, pqn = "P%d" % ps, "P%d" % pq

            def mm_sum(t, c_=c):
                r = None
                for kt in range(NT):
                    r = t.matmul(P[ps][:], onesF[:], xnb[c_][:, kt * RK:(kt + 1) * RK],
                                 start=(kt == 0), stop=(kt == NT - 1))
                return r
            mk = B.emit("t", B.dep_r("onesF", *xname) + B.dep_w(psn), mm_sum, "T")
            B.did_w(mk, psn); B.did_r(mk, "onesF", *xname)
            for kt in range(NT):
                p = kt % 2
                mk = B.emit("v", B.dep_r(xname[kt]) + B.dep_w("sq%d" % p),
                            lambda v, k_=kt, p_=p, c_=c: v.tensor_tensor(
                                sqr[:, p_ * RK:(p_ + 1) * RK], xnb[c_][:, k_ * RK:(k_ + 1) * RK],
                                xnb[c_][:, k_ * RK:(k_ + 1) * RK], op=ALU.mult), "V")
                B.did_w(mk, "sq%d" % p); B.did_r(mk, xname[kt])
                mk2 = B.emit("t", B.dep_r("sq%d" % p, "onesF") + (B.dep_w(pqn) if kt == 0 else []),
                             lambda t, k_=kt, p_=p: t.matmul(P[pq][:], onesF[:], sqr[:, p_ * RK:(p_ + 1) * RK],
                                                             start=(k_ == 0), stop=(k_ == NT - 1)), "T")
                B.did_r(mk2, "sq%d" % p, "onesF")
                if kt == NT - 1:
                    B.did_w(mk2, pqn)
            mk = B.emit("v", B.dep_r(psn) + B.dep_w("mb%d" % c),
                        lambda v, c_=c: v.tensor_scalar_mul(mbuf[c_][:], P[ps][:], 1.0 / D), "V")
            B.did_w(mk, "mb%d" % c); B.did_r(mk, psn)
            mk = B.emit("v", B.dep_r("mb%d" % c) + B.dep_w("rsd%d" % c),
                        lambda v, c_=c: v.tensor_tensor(rsdb[c_][:], mbuf[c_][:], mbuf[c_][:], op=ALU.mult), "V")
            B.did_r(mk, "mb%d" % c)
            mk = B.emit("v", [mk] + B.dep_r(pqn),
                        lambda v, c_=c: v.scalar_tensor_tensor(
                            rsdb[c_][:], P[pq][:], 1.0 / D, rsdb[c_][:],
                            op0=ALU.mult, op1=ALU.subtract), "V")
            B.did_r(mk, pqn)
            mk = B.emit("v", [mk],
                        lambda v, c_=c: v.tensor_scalar_add(rsdb[c_][:], rsdb[c_][:], EPS), "V")
            B.did_w(mk, "rsd%d" % c)
            mk = B.emit("s", B.dep_w("rsd%d" % c),
                        lambda s, c_=c: s.activation(rsdb[c_][:], rsdb[c_][:], AF.Sqrt), "S")
            B.did_w(mk, "rsd%d" % c)
            mk = B.emit("v", B.dep_w("rsd%d" % c),
                        lambda v, c_=c: v.reciprocal(rsdb[c_][:], rsdb[c_][:]), "V")
            B.did_w(mk, "rsd%d" % c)
            for kt in range(NT):
                mk = B.emit("v", B.dep_w(xname[kt]) + B.dep_r("mb%d" % c),
                            lambda v, k_=kt, c_=c: v.tensor_tensor(
                                xnb[c_][:, k_ * RK:(k_ + 1) * RK], xnb[c_][:, k_ * RK:(k_ + 1) * RK],
                                mbuf[c_][:], op=ALU.subtract), "V")
                B.did_r(mk, "mb%d" % c)
                mk = B.emit("v", [mk] + B.dep_r("rsd%d" % c),
                            lambda v, k_=kt, c_=c: v.tensor_tensor(
                                xnb[c_][:, k_ * RK:(k_ + 1) * RK], xnb[c_][:, k_ * RK:(k_ + 1) * RK],
                                rsdb[c_][:], op=ALU.mult), "V")
                B.did_w(mk, xname[kt]); B.did_r(mk, "rsd%d" % c)

        def kv_unit(mt):
            wsl = mt % 4
            for c in (0, 1):
                pb, pbn = P[3 + c], "P%d" % (3 + c)
                kvn = "kv%d_t%d" % (c, mt)

                def mm(t, p_=pb, w_=wbuf[wsl], c_=c):
                    r = None
                    for kt in range(NT):
                        r = t.matmul(p_[:], w_[:, kt * 128:(kt + 1) * 128],
                                     xnb[c_][:, kt * RK:(kt + 1) * RK],
                                     start=(kt == 0), stop=(kt == NT - 1))
                    return r
                mk = B.emit("t", B.dep_r("wb%d" % wsl, *XN[c]) + B.dep_w(pbn), mm, "T")
                B.did_w(mk, pbn); B.did_r(mk, "wb%d" % wsl, *XN[c])
                mk = B.emit("s", B.dep_r(pbn) + B.dep_w(kvn),
                            lambda s_, m_=mt, p_=pb, c_=c: s_.copy(
                                kvb[c_][:, m_ * RK:(m_ + 1) * RK], p_[:]), "S")
                B.did_r(mk, pbn); B.did_w(mk, kvn)
            for c in (0, 1):
                kvn = "kv%d_t%d" % (c, mt)
                j = c
                mk = B.emit("v", B.dep_r(kvn) + B.dep_w("sq%d" % j),
                            lambda v, m_=mt, j_=j, c_=c: v.tensor_tensor(
                                sqr[:, j_ * RK:(j_ + 1) * RK], kvb[c_][:, m_ * RK:(m_ + 1) * RK],
                                kvb[c_][:, m_ * RK:(m_ + 1) * RK], op=ALU.mult), "V")
                B.did_w(mk, "sq%d" % j); B.did_r(mk, kvn)
                mk = B.emit("t", B.dep_r("sq%d" % j, "onesF") + B.dep_w("P5"),
                            lambda t, j_=j: t.matmul(
                                P[5][:], onesF[:], sqr[:, j_ * RK:(j_ + 1) * RK],
                                start=True, stop=True), "T")
                B.did_w(mk, "P5"); B.did_r(mk, "sq%d" % j, "onesF")
                mk = B.emit("s", B.dep_r("P5") + B.dep_w("rcp%d" % j),
                            lambda s_, j_=j: s_.activation(
                                rcp[:, j_ * RK:(j_ + 1) * RK], P[5][:], AF.Sqrt, scale=C_KV), "S")
                B.did_w(mk, "rcp%d" % j); B.did_r(mk, "P5")
                mk = B.emit("v", B.dep_w("rcp%d" % j),
                            lambda v, j_=j: v.reciprocal(
                                rcp[:, j_ * RK:(j_ + 1) * RK], rcp[:, j_ * RK:(j_ + 1) * RK]), "V")
                B.did_w(mk, "rcp%d" % j)
                mk = B.emit("v", B.dep_w(kvn) + B.dep_r("rcp%d" % j),
                            lambda v, m_=mt, j_=j, c_=c: v.tensor_tensor(
                                kvb[c_][:, m_ * RK:(m_ + 1) * RK], kvb[c_][:, m_ * RK:(m_ + 1) * RK],
                                rcp[:, j_ * RK:(j_ + 1) * RK], op=ALU.mult), "V")
                B.did_w(mk, kvn); B.did_r(mk, "rcp%d" % j)
                mk = B.emit("v", B.dep_r(kvn) + B.dep_w("tot%d" % c),
                            lambda v, m_=mt, c_=c: v.tensor_reduce(
                                tots[c_][:, m_:m_ + 1], kvb[c_][:, m_ * RK:(m_ + 1) * RK],
                                mybir.AxisListType.X, ALU.add), "V")
                B.did_r(mk, kvn)
                if mt == NT - 1:
                    B.did_w(mk, "tot%d" % c)
                else:
                    B.did_w(mk, "totpart%d" % c)
            if mt + 4 < NT:
                dma_wtile(wkv, mt + 4, wbuf, "wb", "DW", nring=4)

        def carry_exchange():
            mk = B.emit("v", B.dep_r("tot0", "tot1") + B.dep_w("ctot"),
                        lambda v: v.tensor_tensor(ctot[:], tots[0][:], tots[1][:], op=ALU.add), "V")
            B.did_w(mk, "ctot"); B.did_r(mk, "tot0", "tot1")
            mk = B.emit("q", B.dep_r("ctot") + B.dep_w("cc_in"),
                        lambda q: q.dma_start(out=cc_in[:, :], in_=ctot[:]), "DI")
            B.did_r(mk, "ctot"); B.did_w(mk, "cc_in")
            mk = B.emit("g", B.dep_r("cc_in") + B.dep_w("cc_out"),
                        lambda g: g.collective_compute(
                            "AllGather", ALU.bypass,
                            replica_groups=[list(range(8))],
                            ins=[cc_in.ap().opt()], outs=[cc_out.ap().opt()]), "CC")
            B.did_r(mk, "cc_in"); B.did_w(mk, "cc_out")
            mk = B.emit("q", B.dep_r("cc_out") + B.dep_w("gth"),
                        lambda q: q.dma_start(
                            out=gth[:].rearrange("p (r t) -> p r t", r=8),
                            in_=cc_out.ap().rearrange("(r p) t -> p r t", p=128)), "DH")
            B.did_w(mk, "gth"); B.did_r(mk, "cc_out")

        def carry_combine():
            for r in range(8):
                mk = B.emit("v", B.dep_r("gth", "mcb") + B.dep_w("carA"),
                            lambda v, r_=r: v.scalar_tensor_tensor(
                                carry[0][:], gth[:, r_ * 16:(r_ + 1) * 16], mcb[:, r_:r_ + 1],
                                carry[0][:], op0=ALU.mult, op1=ALU.add), "V")
                B.did_w(mk, "carA"); B.did_r(mk, "gth", "mcb")
            mk = B.emit("v", B.dep_r("carA", "tot0") + B.dep_w("car1"),
                        lambda v: v.tensor_tensor(carry[1][:], carry[0][:], tots[0][:], op=ALU.add), "V")
            B.did_w(mk, "car1"); B.did_r(mk, "carA", "tot0")

        def scan_step(i, steps):
            mt, c = steps[i]
            kvn = "kv%d_t%d" % (c, mt)
            cn = "carA" if c == 0 else "car1"
            mk = B.emit("v", B.dep_w(kvn) + B.dep_r(cn, "onesS"),
                        lambda v, m_=mt, c_=c: v.tensor_tensor_scan(
                            kvb[c_][:, m_ * RK:(m_ + 1) * RK], onesS[:],
                            kvb[c_][:, m_ * RK:(m_ + 1) * RK],
                            carry[c_][:, m_:m_ + 1],
                            op0=ALU.mult, op1=ALU.add), "V")
            B.did_w(mk, kvn); B.did_r(mk, cn, "onesS")

        def fg_unit(c, mt, bg, bu):
            i2 = mt % 16
            par = mt % 2
            pg, pgn = P[bg], "P%d" % bg
            pu, pun = P[bu], "P%d" % bu

            def mmg(t, p_=pg, w_=wgb[mt % 2], c_=c):
                r = None
                for kt in range(NT):
                    r = t.matmul(p_[:], w_[:, kt * 128:(kt + 1) * 128],
                                 xnb[c_][:, kt * RK:(kt + 1) * RK],
                                 start=(kt == 0), stop=(kt == NT - 1))
                return r
            mk = B.emit("t", B.dep_r("wgb%d" % (mt % 2), *XN[c]) + B.dep_w(pgn), mmg, "T")
            B.did_w(mk, pgn); B.did_r(mk, "wgb%d" % (mt % 2), *XN[c])
            if mt + 2 < 64:
                dma_wtile(wg, mt + 2, wgb, "wgb", "DG")

            def mmu(t, p_=pu, w_=wub[mt % 2], c_=c):
                r = None
                for kt in range(NT):
                    r = t.matmul(p_[:], w_[:, kt * 128:(kt + 1) * 128],
                                 xnb[c_][:, kt * RK:(kt + 1) * RK],
                                 start=(kt == 0), stop=(kt == NT - 1))
                return r
            mk = B.emit("t", B.dep_r("wub%d" % (mt % 2), *XN[c]) + B.dep_w(pun), mmu, "T")
            B.did_w(mk, pun); B.did_r(mk, "wub%d" % (mt % 2), *XN[c])
            if mt + 2 < 64:
                dma_wtile(wu, mt + 2, wub, "wub", "DU")
            mk = B.emit("s", B.dep_r(pgn, "sgv") + B.dep_w("silg%d" % par),
                        lambda s, m_=mt, p_=pg, g_=silg[par]: s.activation(
                            g_[:], p_[:], AF.Silu, scale=sgv[:, m_:m_ + 1]), "S")
            B.did_r(mk, pgn, "sgv"); B.did_w(mk, "silg%d" % par)
            mk = B.emit("v", B.dep_r(pun, "suv", "silg%d" % par) + B.dep_w("fuse_t%d" % i2),
                        lambda v, m_=mt, i_=i2, p_=pu, g_=silg[par]: v.scalar_tensor_tensor(
                            fuse[:, i_ * RK:(i_ + 1) * RK], p_[:], suv[:, m_:m_ + 1], g_[:],
                            op0=ALU.mult, op1=ALU.mult), "V")
            B.did_r(mk, pun, "suv", "silg%d" % par); B.did_w(mk, "fuse_t%d" % i2)

        def fd_unit(grp, ot, bd):
            idx = grp * 16 + ot
            wsl = idx % 2
            pd, pdn = P[bd], "P%d" % bd
            fdeps = [B.dep_r("fuse_t%d" % i)[0] for i in range(16)]
            deps = B.dep_r("wdb%d" % wsl) + fdeps + B.dep_w(pdn)

            def mmd(t, p_=pd, w_=wdb[wsl]):
                r = None
                for i in range(16):
                    r = t.matmul(p_[:], w_[:, i * 128:(i + 1) * 128],
                                 fuse[:, i * RK:(i + 1) * RK],
                                 start=(i == 0), stop=(i == 15))
                return r
            mk = B.emit("t", deps, mmd, "T")
            B.did_w(mk, pdn)
            B.did_r(mk, "wdb%d" % wsl, *["fuse_t%d" % i for i in range(16)])
            if idx + 2 < 64:
                dma_wd_piece(idx + 2)
            if grp == 0:
                mk = B.emit("s", B.dep_r(pdn) + B.dep_w("acc_t%d" % ot),
                            lambda s, o_=ot, p_=pd: s.copy(acc[:, o_ * RK:(o_ + 1) * RK], p_[:]), "S")
            else:
                mk = B.emit("v", B.dep_r(pdn, "acc_t%d" % ot),
                            lambda v, o_=ot, p_=pd: v.tensor_tensor(
                                acc[:, o_ * RK:(o_ + 1) * RK], acc[:, o_ * RK:(o_ + 1) * RK],
                                p_[:], op=ALU.add), "V")
            B.did_r(mk, pdn); B.did_w(mk, "acc_t%d" % ot)

        def epilogue(c):
            outcol = c * RK
            for ot in range(NT):
                par = ot % 2
                mk = B.emit("q", B.dep_w("ard%d" % par) + B.dep_r("ascr%d_t%d" % (c, ot)),
                            lambda q, o_=ot, c_=c, a_=a_rd[par]: q.dma_start(
                                out=a_[:], in_=ascr[o_ * 128:(o_ + 1) * 128, c_ * RK:(c_ + 1) * RK]), "DB%d" % par)
                B.did_w(mk, "ard%d" % par); B.did_r(mk, "ascr%d_t%d" % (c, ot))
                mk = B.emit("g", B.dep_w("xrd%d" % par),
                            lambda g, o_=ot, x_=x_rd[par]: g.dma_start(
                                out=x_[:], in_=x[o_ * 128:(o_ + 1) * 128, outcol:outcol + RK]),
                            "DB%d" % par)
                B.did_w(mk, "xrd%d" % par)
                mk = B.emit("v", B.dep_r("ard%d" % par, "xrd%d" % par) + B.dep_w("axr%d" % par),
                            lambda v, a_=a_rd[par], x_=x_rd[par], r_=axr[par]: v.tensor_tensor(
                                r_[:], a_[:], x_[:], op=ALU.add), "V")
                B.did_r(mk, "ard%d" % par, "xrd%d" % par); B.did_w(mk, "axr%d" % par)
                mk = B.emit("v", B.dep_r("acc_t%d" % ot, "sdv", "axr%d" % par) + B.dep_w("ost%d" % par),
                            lambda v, o_=ot, r_=axr[par], u_=outst[par]: v.scalar_tensor_tensor(
                                u_[:], acc[:, o_ * RK:(o_ + 1) * RK], sdv[:, o_:o_ + 1], r_[:],
                                op0=ALU.mult, op1=ALU.add), "V")
                B.did_r(mk, "acc_t%d" % ot, "sdv", "axr%d" % par); B.did_w(mk, "ost%d" % par)
                mk = B.emit("q", B.dep_r("ost%d" % par),
                            lambda q, o_=ot, u_=outst[par]: q.dma_start(
                                out=out[o_ * 128:(o_ + 1) * 128, outcol:outcol + RK], in_=u_[:]), "DO")
                B.did_r(mk, "ost%d" % par)

        def q_dft_phase():
            steps = [(mt, c) for mt in range(NT) for c in (0, 1)]
            DELAY = 3

            def qproj_mm(i):
                mt, c = steps[i]
                wsl = mt % 4
                pb, pbn = P[3 + c], "P%d" % (3 + c)
                j = i % 4
                jr = i % 2
                qn = "qr%d" % j

                def mm(t, p_=pb, w_=wbuf[wsl], c_=c):
                    r = None
                    for kt in range(NT):
                        r = t.matmul(p_[:], w_[:, kt * 128:(kt + 1) * 128],
                                     xnb[c_][:, kt * RK:(kt + 1) * RK],
                                     start=(kt == 0), stop=(kt == NT - 1))
                    return r
                mk = B.emit("t", B.dep_r("wb%d" % wsl, *XN[c]) + B.dep_w(pbn), mm, "T")
                B.did_w(mk, pbn); B.did_r(mk, "wb%d" % wsl, *XN[c])
                mk = B.emit("s", B.dep_r(pbn) + B.dep_w(qn),
                            lambda s_, j_=j, p_=pb: s_.copy(qr[:, j_ * RK:(j_ + 1) * RK], p_[:]), "S")
                B.did_r(mk, pbn); B.did_w(mk, qn)
                mk = B.emit("v", B.dep_r(qn) + B.dep_w("sq%d" % jr),
                            lambda v, j_=j, r_=jr: v.tensor_tensor(
                                sqr[:, r_ * RK:(r_ + 1) * RK], qr[:, j_ * RK:(j_ + 1) * RK],
                                qr[:, j_ * RK:(j_ + 1) * RK], op=ALU.mult), "V")
                B.did_w(mk, "sq%d" % jr); B.did_r(mk, qn)

            def qproj_norm(i):
                mt, c = steps[i]
                j = i % 4
                jr = i % 2
                qn = "qr%d" % j
                mk = B.emit("t", B.dep_r("sq%d" % jr, "onesF") + B.dep_w("P5"),
                            lambda t, r_=jr: t.matmul(
                                P[5][:], onesF[:], sqr[:, r_ * RK:(r_ + 1) * RK],
                                start=True, stop=True), "T")
                B.did_w(mk, "P5"); B.did_r(mk, "sq%d" % jr, "onesF")
                mk = B.emit("s", B.dep_r("P5") + B.dep_w("rcp%d" % jr),
                            lambda s_, r_=jr: s_.activation(
                                rcp[:, r_ * RK:(r_ + 1) * RK], P[5][:], AF.Sqrt, scale=C_Q), "S")
                B.did_w(mk, "rcp%d" % jr); B.did_r(mk, "P5")
                mk = B.emit("v", B.dep_w("rcp%d" % jr),
                            lambda v, r_=jr: v.reciprocal(
                                rcp[:, r_ * RK:(r_ + 1) * RK], rcp[:, r_ * RK:(r_ + 1) * RK]), "V")
                B.did_w(mk, "rcp%d" % jr)
                mk = B.emit("v", B.dep_w(qn) + B.dep_r("rcp%d" % jr),
                            lambda v, j_=j, r_=jr: v.tensor_tensor(
                                qr[:, j_ * RK:(j_ + 1) * RK], qr[:, j_ * RK:(j_ + 1) * RK],
                                rcp[:, r_ * RK:(r_ + 1) * RK], op=ALU.mult), "V")
                B.did_w(mk, qn); B.did_r(mk, "rcp%d" % jr)

            def dft(i):
                mt, c = steps[i]
                pb, pbn = P[3 + c], "P%d" % (3 + c)
                j = i % 4
                hp = i % 2
                qn = "qr%d" % j
                kvn = "kv%d_t%d" % (c, mt)
                names = [f"hb{hp}_{k}" for k in range(6)]
                qsl = qr[:, j * RK:(j + 1) * RK]
                ksl = kvb[c][:, mt * RK:(mt + 1) * RK]
                for (src_, srcn, mat, matn, dsti, pp, pn) in (
                        (qsl, qn, cfb, "cfb", 0, P[6], "P6"),
                        (qsl, qn, sfb, "sfb", 1, P[7], "P7"),
                        (ksl, kvn, cfb, "cfb", 2, pb, pbn),
                        (ksl, kvn, sfb, "sfb", 3, P[5], "P5")):
                    mk = B.emit("t", B.dep_r(srcn, matn) + B.dep_w(pn),
                                lambda t, s_=src_, w_=mat, p_=pp: t.matmul(
                                    p_[:], w_[:], s_, start=True, stop=True), "T")
                    B.did_w(mk, pn); B.did_r(mk, srcn, matn)
                    if dsti % 2 == 0:
                        mk = B.emit("v", B.dep_r(pn) + B.dep_w(names[dsti]),
                                    lambda v, b_=hb[hp][dsti], p_=pp: v.tensor_copy(b_[:], p_[:]), "V")
                    else:
                        mk = B.emit("s", B.dep_r(pn) + B.dep_w(names[dsti]),
                                    lambda s_, b_=hb[hp][dsti], p_=pp: s_.copy(b_[:], p_[:]), "S")
                    B.did_r(mk, pn); B.did_w(mk, names[dsti])

                def prods(v, b_=hb[hp]):
                    QC, QS, KC, KS, T5, T6 = b_
                    v.tensor_tensor(T5[:], QC[:], KC[:], op=ALU.mult)
                    v.tensor_tensor(T6[:], QS[:], KS[:], op=ALU.mult)
                    v.tensor_tensor(T5[:], T5[:], T6[:], op=ALU.subtract)
                    v.tensor_tensor(T6[:], QC[:], KS[:], op=ALU.mult)
                    v.tensor_tensor(QC[:], QS[:], KC[:], op=ALU.mult)
                    return v.tensor_tensor(T6[:], T6[:], QC[:], op=ALU.add)
                mk = B.emit("v", B.dep_r(*names) + B.dep_w(names[0], names[4], names[5]), prods, "V")
                B.did_r(mk, *names); B.did_w(mk, names[0], names[4], names[5])
                mk = B.emit("t", B.dep_r(names[4], names[5], "cfb", "sfb") + B.dep_w("P6"),
                            lambda t, b_=hb[hp]: (
                                t.matmul(P[6][:], cfb[:], b_[4][:], start=True, stop=False),
                                t.matmul(P[6][:], sfb[:], b_[5][:], start=False, stop=True))[-1], "T")
                B.did_w(mk, "P6"); B.did_r(mk, names[4], names[5], "cfb", "sfb")
                vhn = "vh%d_t%d" % (c, mt)
                mk = B.emit("s", B.dep_r("P6") + B.dep_w(kvn, vhn),
                            lambda s_, m_=mt, c_=c: s_.copy(
                                kvb[c_][:, m_ * RK:(m_ + 1) * RK], P[6][:]), "S")
                B.did_r(mk, "P6"); B.did_w(mk, kvn); B.did_w(mk, vhn)

            for i in range(len(steps)):
                qproj_mm(i)
                mt, c = steps[i]
                if c == 1 and mt + 4 < NT:
                    dma_wtile(wq, mt + 4, wbuf, "wb", "DW", nring=4)
                if i == 2:
                    carry_combine()
                    for k in range(3):
                        scan_step(k, steps)
                elif i >= 3:
                    scan_step(i, steps)
                if i >= 1:
                    qproj_norm(i - 1)
                if i >= DELAY:
                    dft(i - DELAY)
                if i < 16:
                    fd_unit(0, i, 2)
                else:
                    fg_unit(0, 16 + (i - 16), 0, 1)
            qproj_norm(len(steps) - 1)
            for i in range(len(steps) - DELAY, len(steps)):
                dft(i)

        def wo_unit(o):
            ot, c = o // 2, o % 2
            wsl = ot % 4
            pb, pbn = P[3 + c], "P%d" % (3 + c)
            vh_names = ["vh%d_t%d" % (c, k) for k in range(NT)]

            def mmo(t, p_=pb, w_=wbuf[wsl], c_=c):
                r = None
                for kt in range(NT):
                    r = t.matmul(p_[:], w_[:, kt * 128:(kt + 1) * 128],
                                 kvb[c_][:, kt * RK:(kt + 1) * RK],
                                 start=(kt == 0), stop=(kt == NT - 1))
                return r
            mk = B.emit("t", B.dep_r("wb%d" % wsl, *vh_names) + B.dep_w(pbn), mmo, "T")
            B.did_w(mk, pbn); B.did_r(mk, "wb%d" % wsl, *vh_names)
            an = "ast%d" % c
            mk = B.emit("s", B.dep_r(pbn) + B.dep_w(an),
                        lambda s_, p_=pb, a_=ast[c]: s_.copy(a_[:], p_[:]), "S")
            B.did_r(mk, pbn); B.did_w(mk, an)
            mk = B.emit("q", B.dep_r(an) + B.dep_w("ascr%d_t%d" % (c, ot)),
                        lambda q, o_=ot, c_=c, a_=ast[c]: q.dma_start(
                            out=ascr[o_ * 128:(o_ + 1) * 128, c_ * RK:(c_ + 1) * RK], in_=a_[:]),
                        "DA%d" % c)
            B.did_r(mk, an); B.did_w(mk, "ascr%d_t%d" % (c, ot))
            if c == 1 and ot + 4 < NT:
                dma_wtile(wo, ot + 4, wbuf, "wb", "DW", nring=4)

        ln_chunk(0)
        ln_chunk(1)
        for mt in range(4):
            dma_wtile(wkv, mt, wbuf, "wb", "DW", nring=4)
        dma_wtile(wg, 0, wgb, "wgb", "DG")
        dma_wtile(wu, 0, wub, "wub", "DU")
        dma_wtile(wg, 1, wgb, "wgb", "DG")
        dma_wtile(wu, 1, wub, "wub", "DU")
        for k in range(NT):
            kv_unit(k)
            fg_unit(0, k, 0, 1)
        for mt in range(4):
            dma_wtile(wq, mt, wbuf, "wb", "DW", nring=4)
        dma_wd_piece(0)
        dma_wd_piece(1)
        carry_exchange()
        q_dft_phase()
        for ot in range(4):
            dma_wtile(wo, ot, wbuf, "wb", "DW", nring=4)
        for o in range(32):
            wo_unit(o)
            if o < 16:
                fd_unit(1, o, 2)
            else:
                fg_unit(0, 32 + (o - 16), 0, 1)
        for ot in range(NT):
            fd_unit(2, ot, 4 + ot % 2)
        for j2 in range(NT):
            fg_unit(0, 48 + j2, j2 % 2, 2 + j2 % 2)
        for ot in range(NT):
            fd_unit(3, ot, 4 + ot % 2)
        epilogue(0)
        dma_wtile(wg, 0, wgb, "wgb", "DG")
        dma_wtile(wu, 0, wub, "wub", "DU")
        dma_wtile(wg, 1, wgb, "wgb", "DG")
        dma_wtile(wu, 1, wub, "wub", "DU")
        for grp in range(4):
            for i2 in range(16):
                mt = grp * 16 + i2
                fg_unit(1, mt, mt % 2, 2 + mt % 2)
            if grp == 0:
                dma_wd_piece(0)
                dma_wd_piece(1)
            for ot in range(NT):
                fd_unit(grp, ot, 4 + ot % 2)
        epilogue(1)
        B.emit("q", [("DO", B.cnt["DO"])], lambda q: q.nop(), "G")
        B.flush()
    return nc


_NC_CACHE = None


def kernel(L0_attn_x, L0_attn_W_q, L0_attn_W_kv, L0_attn_W_o,
           L0_ffn_w_gate, L0_ffn_w_up, L0_ffn_w_down,
           L0_ffn_s_gate, L0_ffn_s_up, L0_ffn_s_down):
    x = np.asarray(L0_attn_x, dtype=np.float32)
    Wq = np.asarray(L0_attn_W_q, dtype=np.float32)
    Wkv = np.asarray(L0_attn_W_kv, dtype=np.float32)
    Wo = np.asarray(L0_attn_W_o, dtype=np.float32)
    wg = np.asarray(L0_ffn_w_gate, dtype=np.int8)
    wu = np.asarray(L0_ffn_w_up, dtype=np.int8)
    wd = np.asarray(L0_ffn_w_down, dtype=np.int8)
    sg = np.asarray(L0_ffn_s_gate, dtype=np.float32)
    su = np.asarray(L0_ffn_s_up, dtype=np.float32)
    sdn = np.asarray(L0_ffn_s_down, dtype=np.float32)

    t4a = lambda W: np.ascontiguousarray(W.reshape(16, 128, 16, 128).transpose(0, 3, 2, 1))
    wq_t, wkv_t, wo_t = t4a(Wq), t4a(Wkv), t4a(Wo)
    wg_t = np.ascontiguousarray(wg.reshape(16, 128, 64, 128).transpose(2, 1, 0, 3))
    wu_t = np.ascontiguousarray(wu.reshape(16, 128, 64, 128).transpose(2, 1, 0, 3))
    wd_t = np.ascontiguousarray(wd.reshape(64, 128, 16, 128).transpose(2, 1, 0, 3))
    sg_t = np.ascontiguousarray(sg.reshape(64, 128).T)
    su_t = np.ascontiguousarray(su.reshape(64, 128).T)
    sd_t = np.ascontiguousarray(sdn.reshape(16, 128).T)
    n = np.arange(128)
    ang = 2.0 * np.pi * np.outer(n, n) / 128.0
    cf = np.cos(ang).astype(np.float32)
    sf = np.sin(ang).astype(np.float32)

    in_maps = []
    for c in range(8):
        b, j = c // 4, c % 4
        x_c = np.ascontiguousarray(x[b, j * 1024:(j + 1) * 1024, :].T)
        mrow = np.zeros((8,), dtype=np.float32)
        for r in range(8):
            if r // 4 == b and r % 4 < j:
                mrow[r] = 1.0
        mcb = np.ascontiguousarray(np.tile(mrow, (128, 1)))
        in_maps.append({
            "x": x_c, "wq": wq_t, "wkv": wkv_t, "wo": wo_t,
            "wg": wg_t, "wu": wu_t, "wd": wd_t,
            "sg": sg_t, "su": su_t, "sd": sd_t,
            "cf": cf, "sf": sf, "mc": mcb,
        })

    global _NC_CACHE
    if _NC_CACHE is None:
        _NC_CACHE = build_program()
    res = run_bass_kernel_spmd(_NC_CACHE, in_maps, core_ids=list(range(8)))
    outp = np.empty((2, 4096, D), dtype=np.float32)
    for c in range(8):
        b, j = c // 4, c % 4
        outp[b, j * 1024:(j + 1) * 1024, :] = np.asarray(res.results[c]["out"]).T
    return outp


# version 14
# speedup vs baseline: 1.4369x; 1.4369x over previous
import contextlib
import numpy as np
import concourse.bass as bass
import concourse.mybir as mybir
from concourse.bass_utils import run_bass_kernel_spmd

F32 = mybir.dt.float32
BF16 = mybir.dt.bfloat16
I8 = mybir.dt.int8
AF = mybir.ActivationFunctionType
ALU = mybir.AluOpType

D = 2048
NT = 16
RK = 512
ROWS = 1024
C_Q = float(128 ** 3)
C_KV = 128.0
EPS = 1e-5

_EK = ("t", "v", "s", "g", "q")


class Bld:
    def __init__(self, nc, block, semnames, es):
        self.nc = nc
        self.sec = {"t": block.tensor, "v": block.vector, "s": block.scalar,
                    "g": block.gpsimd, "q": block.sync}
        self.sem, self.cnt = {}, {}
        self.seen = {k: {} for k in _EK}
        for n in semnames:
            self.sem[n] = es.enter_context(nc.semaphore(n))
            self.cnt[n] = 0
        self.wr, self.rd = {}, {}
        self.pend = {}

    def emit(self, ekey, deps, fn, inc):
        step = 16 if inc.startswith("D") else 1
        todo, seen = [], self.seen[ekey]
        for d in deps:
            if d is None:
                continue
            s, v = d
            if v > 0 and seen.get(s, 0) < v:
                todo.append((s, v))
                seen[s] = v
        semh, inc_sem = self.sem, self.sem[inc]

        def body(eng):
            for s, v in todo:
                eng.wait_ge(semh[s], v)
            fn(eng).then_inc(inc_sem, step)

        self.pend.setdefault(ekey, []).append(body)
        self.cnt[inc] += step
        return (inc, self.cnt[inc])

    def flush(self):
        for ekey, bodies in self.pend.items():
            def run_all(eng, bodies_=bodies):
                for b in bodies_:
                    b(eng)
            self.sec[ekey](run_all)
        self.pend = {}

    def dep_r(self, *bufs):
        return [self.wr.get(b) for b in bufs]

    def dep_w(self, *bufs):
        out = []
        for b in bufs:
            out.extend(self.rd.get(b, {}).items())
            if b in self.wr:
                out.append(self.wr[b])
        return out

    def did_r(self, mark, *bufs):
        for b in bufs:
            d = self.rd.setdefault(b, {})
            if d.get(mark[0], 0) < mark[1]:
                d[mark[0]] = mark[1]

    def did_w(self, mark, *bufs):
        for b in bufs:
            self.wr[b] = mark
            self.rd.pop(b, None)


def build_program():
    nc = bass.Bass("TRN2", target_bir_lowering=False)
    dti = lambda n, s, d: nc.dram_tensor(n, s, d, kind="ExternalInput").ap()
    x = dti("x", [D, ROWS], F32)
    wq = dti("wq", [16, 128, 16, 128], F32)
    wkv = dti("wkv", [16, 128, 16, 128], F32)
    wo = dti("wo", [16, 128, 16, 128], F32)
    wg = dti("wg", [64, 128, 16, 128], I8)
    wu = dti("wu", [64, 128, 16, 128], I8)
    wd = dti("wd", [16, 128, 64, 128], I8)
    sg = dti("sg", [128, 64], F32)
    su = dti("su", [128, 64], F32)
    sd = dti("sd", [128, 16], F32)
    cfm = dti("cf", [128, 128], F32)
    sfm = dti("sf", [128, 128], F32)
    mc = dti("mc", [128, 8], F32)
    out = nc.dram_tensor("out", [D, ROWS], F32, kind="ExternalOutput").ap()
    ascr = nc.dram_tensor("ascr", [D, ROWS], BF16, kind="Internal").ap()
    cc_in = nc.dram_tensor("cc_in", [128, 16], F32, kind="Internal")
    cc_out = nc.dram_tensor("cc_out", [8 * 128, 16], F32, kind="Internal")

    sems = ["T", "V", "S", "G", "CC", "DC",
            "DX0", "DX1", "DW0", "DW1", "DG0", "DG1", "DU0", "DU1",
            "DD0", "DD1", "DA0", "DA1", "DB0", "DB1",
            "DI", "DH", "DO"]

    es = contextlib.ExitStack()
    with es:
        sb = lambda n, s, d: es.enter_context(nc.sbuf_tensor(n, s, d))
        xnb = [sb("xn0", [128, NT * RK], BF16), sb("xn1", [128, NT * RK], BF16)]
        kvb = [sb("kv0", [128, NT * RK], BF16), sb("kv1", [128, NT * RK], BF16)]
        qr = sb("qr", [128, 4 * RK], BF16)
        sqr = sb("sqr", [128, 4 * RK], BF16)
        rcp = sb("rcp", [128, 2 * RK], F32)
        mbuf = [sb("mb0", [128, RK], F32), sb("mb1", [128, RK], F32)]
        rsdb = [sb("rsd0", [128, RK], F32), sb("rsd1", [128, RK], F32)]
        wdb = [sb("wdb%d" % i, [128, 2048], BF16) for i in range(3)]
        hb = [[sb(f"hb{p}_{i}", [128, RK], BF16) for i in range(6)] for p in range(2)]
        wbuf = [sb("wb%d" % i, [128, 2048], BF16) for i in range(4)]
        wgb = [sb("wgb0", [128, 2048], BF16), sb("wgb1", [128, 2048], BF16)]
        wub = [sb("wub0", [128, 2048], BF16), sb("wub1", [128, 2048], BF16)]
        fuse = sb("fuse", [128, 16 * RK], BF16)
        silg = [sb("silg0", [128, RK], BF16), sb("silg1", [128, RK], BF16)]
        acc = sb("acc", [128, NT * RK], F32)
        ast = [sb("ast0", [128, RK], BF16), sb("ast1", [128, RK], BF16)]
        a_rd = [sb("ard0", [128, RK], BF16), sb("ard1", [128, RK], BF16)]
        x_rd = [sb("xrd0", [128, RK], BF16), sb("xrd1", [128, RK], BF16)]
        axr = [sb("axr0", [128, RK], F32), sb("axr1", [128, RK], F32)]
        outst = [sb("ost0", [128, RK], F32), sb("ost1", [128, RK], F32)]
        tots = [sb("tot0", [128, 16], F32), sb("tot1", [128, 16], F32)]
        ctot = sb("ctot", [128, 16], F32)
        carry = [sb("carA", [128, 16], F32), sb("car1", [128, 16], F32)]
        gth = sb("gth", [128, 128], F32)
        sgv = sb("sgv", [128, 64], F32)
        suv = sb("suv", [128, 64], F32)
        sdv = sb("sdv", [128, 16], F32)
        cfb = sb("cfb", [128, 128], BF16)
        sfb = sb("sfb", [128, 128], BF16)
        mcb = sb("mcb", [128, 8], F32)
        onesF = sb("onesF", [128, 128], BF16)
        onesS = sb("onesS", [128, RK], BF16)
        P = [es.enter_context(nc.psum_tensor(f"P{i}", [128, 512], F32)) for i in range(8)]
        print('SBUF remaining after alloc:', nc.sbuf_bytes_remaining)
        block = es.enter_context(nc.Block())
        B = Bld(nc, block, sems, es)


        def prolog(g):
            g.memset(onesF[:], 1.0)
            g.memset(onesS[:], 1.0)
            return g.memset(carry[0][:], 0.0)
        mk = B.emit("g", [], prolog, "G")
        B.did_w(mk, "onesF", "onesS", "carA")
        for dst, src_, nm in ((sgv, sg, "sgv"), (suv, su, "suv"), (sdv, sd, "sdv"),
                              (cfb, cfm, "cfb"), (sfb, sfm, "sfb"), (mcb, mc, "mcb")):
            mk = B.emit("g", [], (lambda d_, s_: (lambda g: g.dma_start(out=d_[:], in_=s_)))(dst, src_), "DC")
            B.did_w(mk, nm)
        for c in (0, 1):
            mk = B.emit("g", [],
                        lambda g, c_=c: g.dma_start(
                            out=xnb[c_][:, 0:NT * RK].rearrange("p (t r) -> p t r", t=NT),
                            in_=x[:, c_ * RK:(c_ + 1) * RK].rearrange("(t p) r -> p t r", p=128)),
                        "DX%d" % c)
            B.did_w(mk, *["xn%d_t%d" % (c, t) for t in range(NT)])

        XN = [["xn%d_t%d" % (c, t) for t in range(NT)] for c in (0, 1)]

        def dma_wtile(wsrc, mt, bufs, bname, sem_base, nring=2):
            sl = mt % nring
            nm = bname + str(sl)
            mk = B.emit("g", B.dep_w(nm),
                        lambda g, w_=wsrc, m_=mt, d_=bufs[sl]: g.dma_start(
                            out=d_[:], in_=w_[m_].rearrange("p a b -> p (a b)")),
                        "%s%d" % (sem_base, sl % 2))
            B.did_w(mk, nm)

        def dma_wd_piece(idx):
            ot, grp, sl = idx % 16, idx // 16, idx % 3
            nm = "wdb%d" % sl
            mk = B.emit("g", B.dep_w(nm),
                        lambda g, o_=ot, p_=grp, d_=wdb[sl]: g.dma_start(
                            out=d_[:], in_=wd[o_, :, p_ * 16:(p_ + 1) * 16, :].rearrange("p a b -> p (a b)")),
                        "DD0")
            B.did_w(mk, nm)

        def ln_chunk(c):
            xname = XN[c]
            ps, pq = 6, 7
            psn, pqn = "P%d" % ps, "P%d" % pq

            def mm_sum(t, c_=c):
                r = None
                for kt in range(NT):
                    r = t.matmul(P[ps][:], onesF[:], xnb[c_][:, kt * RK:(kt + 1) * RK],
                                 start=(kt == 0), stop=(kt == NT - 1))
                return r
            mk = B.emit("t", B.dep_r("onesF", *xname) + B.dep_w(psn), mm_sum, "T")
            B.did_w(mk, psn); B.did_r(mk, "onesF", *xname)
            for kt in range(NT):
                p = kt % 2
                mk = B.emit("s", B.dep_r(xname[kt]) + B.dep_w("sq%d" % p),
                            lambda s_, k_=kt, p_=p, c_=c: s_.square(
                                sqr[:, p_ * RK:(p_ + 1) * RK],
                                xnb[c_][:, k_ * RK:(k_ + 1) * RK]), "S")
                B.did_w(mk, "sq%d" % p); B.did_r(mk, xname[kt])
                mk2 = B.emit("t", B.dep_r("sq%d" % p, "onesF") + (B.dep_w(pqn) if kt == 0 else []),
                             lambda t, k_=kt, p_=p: t.matmul(P[pq][:], onesF[:], sqr[:, p_ * RK:(p_ + 1) * RK],
                                                             start=(k_ == 0), stop=(k_ == NT - 1)), "T")
                B.did_r(mk2, "sq%d" % p, "onesF")
                if kt == NT - 1:
                    B.did_w(mk2, pqn)
            mk = B.emit("v", B.dep_r(psn) + B.dep_w("mb%d" % c),
                        lambda v, c_=c: v.tensor_scalar_mul(mbuf[c_][:], P[ps][:], 1.0 / D), "V")
            B.did_w(mk, "mb%d" % c); B.did_r(mk, psn)
            mk = B.emit("v", B.dep_r("mb%d" % c) + B.dep_w("rsd%d" % c),
                        lambda v, c_=c: v.tensor_tensor(rsdb[c_][:], mbuf[c_][:], mbuf[c_][:], op=ALU.mult), "V")
            B.did_r(mk, "mb%d" % c)
            mk = B.emit("v", [mk] + B.dep_r(pqn),
                        lambda v, c_=c: v.scalar_tensor_tensor(
                            rsdb[c_][:], P[pq][:], 1.0 / D, rsdb[c_][:],
                            op0=ALU.mult, op1=ALU.subtract), "V")
            B.did_r(mk, pqn)
            mk = B.emit("v", [mk],
                        lambda v, c_=c: v.tensor_scalar_add(rsdb[c_][:], rsdb[c_][:], EPS), "V")
            B.did_w(mk, "rsd%d" % c)
            mk = B.emit("s", B.dep_w("rsd%d" % c),
                        lambda s, c_=c: s.activation(rsdb[c_][:], rsdb[c_][:], AF.Sqrt), "S")
            B.did_w(mk, "rsd%d" % c)
            mk = B.emit("v", B.dep_w("rsd%d" % c),
                        lambda v, c_=c: v.reciprocal(rsdb[c_][:], rsdb[c_][:]), "V")
            B.did_w(mk, "rsd%d" % c)
            ne, nsem = ("v", "V") if c == 0 else ("g", "G")
            for kt in range(NT):
                mk = B.emit(ne, B.dep_w(xname[kt]) + B.dep_r("mb%d" % c),
                            lambda v, k_=kt, c_=c: v.tensor_tensor(
                                xnb[c_][:, k_ * RK:(k_ + 1) * RK], xnb[c_][:, k_ * RK:(k_ + 1) * RK],
                                mbuf[c_][:], op=ALU.subtract), nsem)
                B.did_r(mk, "mb%d" % c)
                mk = B.emit(ne, [mk] + B.dep_r("rsd%d" % c),
                            lambda v, k_=kt, c_=c: v.tensor_tensor(
                                xnb[c_][:, k_ * RK:(k_ + 1) * RK], xnb[c_][:, k_ * RK:(k_ + 1) * RK],
                                rsdb[c_][:], op=ALU.mult), nsem)
                B.did_w(mk, xname[kt]); B.did_r(mk, "rsd%d" % c)

        def kv_mm_unit(mt):
            wsl = mt % 4
            for c in (0, 1):
                pb, pbn = P[3 + c], "P%d" % (3 + c)
                kvn = "kv%d_t%d" % (c, mt)

                def mm(t, p_=pb, w_=wbuf[wsl], c_=c):
                    r = None
                    for kt in range(NT):
                        r = t.matmul(p_[:], w_[:, kt * 128:(kt + 1) * 128],
                                     xnb[c_][:, kt * RK:(kt + 1) * RK],
                                     start=(kt == 0), stop=(kt == NT - 1))
                    return r
                mk = B.emit("t", B.dep_r("wb%d" % wsl, *XN[c]) + B.dep_w(pbn), mm, "T")
                B.did_w(mk, pbn); B.did_r(mk, "wb%d" % wsl, *XN[c])
                mk = B.emit("s", B.dep_r(pbn) + B.dep_w(kvn),
                            lambda s_, m_=mt, p_=pb, c_=c: s_.copy(
                                kvb[c_][:, m_ * RK:(m_ + 1) * RK], p_[:]), "S")
                B.did_r(mk, pbn); B.did_w(mk, kvn)
                jq = 2 * (mt % 2) + c
                mk = B.emit("s", B.dep_r(kvn) + B.dep_w("sq%d" % jq),
                            lambda s_, m_=mt, j_=jq, c_=c: s_.square(
                                sqr[:, j_ * RK:(j_ + 1) * RK],
                                kvb[c_][:, m_ * RK:(m_ + 1) * RK]), "S")
                B.did_w(mk, "sq%d" % jq); B.did_r(mk, kvn)
            if mt + 4 < NT:
                dma_wtile(wkv, mt + 4, wbuf, "wb", "DW", nring=4)

        def kv_norm_unit(mt):
            for c in (0, 1):
                kvn = "kv%d_t%d" % (c, mt)
                j = 2 * (mt % 2) + c
                nb, nbn = (P[5], "P5") if c == 0 else (P[6], "P6")
                mk = B.emit("t", B.dep_r("sq%d" % j, "onesF") + B.dep_w(nbn),
                            lambda t, j_=j, n_=nb: t.matmul(
                                n_[:], onesF[:], sqr[:, j_ * RK:(j_ + 1) * RK],
                                start=True, stop=True), "T")
                B.did_w(mk, nbn); B.did_r(mk, "sq%d" % j, "onesF")
                mk = B.emit("s", B.dep_r(nbn) + B.dep_w("rcp%d" % c),
                            lambda s_, j_=c, n_=nb: s_.activation(
                                rcp[:, j_ * RK:(j_ + 1) * RK], n_[:], AF.Sqrt, scale=C_KV), "S")
                B.did_w(mk, "rcp%d" % c); B.did_r(mk, nbn)
                mk = B.emit("v", B.dep_w("rcp%d" % c),
                            lambda v, j_=c: v.reciprocal(
                                rcp[:, j_ * RK:(j_ + 1) * RK], rcp[:, j_ * RK:(j_ + 1) * RK]), "V")
                B.did_w(mk, "rcp%d" % c)
                mk = B.emit("v", B.dep_w(kvn) + B.dep_r("rcp%d" % c),
                            lambda v, m_=mt, j_=c, c_=c: v.tensor_tensor(
                                kvb[c_][:, m_ * RK:(m_ + 1) * RK], kvb[c_][:, m_ * RK:(m_ + 1) * RK],
                                rcp[:, j_ * RK:(j_ + 1) * RK], op=ALU.mult), "V")
                B.did_w(mk, kvn); B.did_r(mk, "rcp%d" % c)
                mk = B.emit("v", B.dep_r(kvn) + B.dep_w("tot%d" % c),
                            lambda v, m_=mt, c_=c: v.tensor_reduce(
                                tots[c_][:, m_:m_ + 1], kvb[c_][:, m_ * RK:(m_ + 1) * RK],
                                mybir.AxisListType.X, ALU.add), "V")
                B.did_r(mk, kvn)
                if mt == NT - 1:
                    B.did_w(mk, "tot%d" % c)
                else:
                    B.did_w(mk, "totpart%d" % c)

        def carry_exchange():
            mk = B.emit("v", B.dep_r("tot0", "tot1") + B.dep_w("ctot"),
                        lambda v: v.tensor_tensor(ctot[:], tots[0][:], tots[1][:], op=ALU.add), "V")
            B.did_w(mk, "ctot"); B.did_r(mk, "tot0", "tot1")
            mk = B.emit("q", B.dep_r("ctot") + B.dep_w("cc_in"),
                        lambda q: q.dma_start(out=cc_in[:, :], in_=ctot[:]), "DI")
            B.did_r(mk, "ctot"); B.did_w(mk, "cc_in")
            mk = B.emit("g", B.dep_r("cc_in") + B.dep_w("cc_out"),
                        lambda g: g.collective_compute(
                            "AllGather", ALU.bypass,
                            replica_groups=[list(range(8))],
                            ins=[cc_in.ap().opt()], outs=[cc_out.ap().opt()]), "CC")
            B.did_r(mk, "cc_in"); B.did_w(mk, "cc_out")
            mk = B.emit("q", B.dep_r("cc_out") + B.dep_w("gth"),
                        lambda q: q.dma_start(
                            out=gth[:].rearrange("p (r t) -> p r t", r=8),
                            in_=cc_out.ap().rearrange("(r p) t -> p r t", p=128)), "DH")
            B.did_w(mk, "gth"); B.did_r(mk, "cc_out")

        def carry_combine():
            for r in range(8):
                mk = B.emit("v", B.dep_r("gth", "mcb") + B.dep_w("carA"),
                            lambda v, r_=r: v.scalar_tensor_tensor(
                                carry[0][:], gth[:, r_ * 16:(r_ + 1) * 16], mcb[:, r_:r_ + 1],
                                carry[0][:], op0=ALU.mult, op1=ALU.add), "V")
                B.did_w(mk, "carA"); B.did_r(mk, "gth", "mcb")
            mk = B.emit("v", B.dep_r("carA", "tot0") + B.dep_w("car1"),
                        lambda v: v.tensor_tensor(carry[1][:], carry[0][:], tots[0][:], op=ALU.add), "V")
            B.did_w(mk, "car1"); B.did_r(mk, "carA", "tot0")

        def scan_step(i, steps):
            mt, c = steps[i]
            kvn = "kv%d_t%d" % (c, mt)
            cn = "carA" if c == 0 else "car1"
            mk = B.emit("v", B.dep_w(kvn) + B.dep_r(cn, "onesS"),
                        lambda v, m_=mt, c_=c: v.tensor_tensor_scan(
                            kvb[c_][:, m_ * RK:(m_ + 1) * RK], onesS[:],
                            kvb[c_][:, m_ * RK:(m_ + 1) * RK],
                            carry[c_][:, m_:m_ + 1],
                            op0=ALU.mult, op1=ALU.add), "V")
            B.did_w(mk, kvn); B.did_r(mk, cn, "onesS")

        def fg_unit(c, mt, bg, bu):
            i2 = mt % 16
            par = mt % 2
            pg, pgn = P[bg], "P%d" % bg
            pu, pun = P[bu], "P%d" % bu

            def mmg(t, p_=pg, w_=wgb[mt % 2], c_=c):
                r = None
                for kt in range(NT):
                    r = t.matmul(p_[:], w_[:, kt * 128:(kt + 1) * 128],
                                 xnb[c_][:, kt * RK:(kt + 1) * RK],
                                 start=(kt == 0), stop=(kt == NT - 1))
                return r
            mk = B.emit("t", B.dep_r("wgb%d" % (mt % 2), *XN[c]) + B.dep_w(pgn), mmg, "T")
            B.did_w(mk, pgn); B.did_r(mk, "wgb%d" % (mt % 2), *XN[c])
            if mt + 2 < 64:
                dma_wtile(wg, mt + 2, wgb, "wgb", "DG")

            def mmu(t, p_=pu, w_=wub[mt % 2], c_=c):
                r = None
                for kt in range(NT):
                    r = t.matmul(p_[:], w_[:, kt * 128:(kt + 1) * 128],
                                 xnb[c_][:, kt * RK:(kt + 1) * RK],
                                 start=(kt == 0), stop=(kt == NT - 1))
                return r
            mk = B.emit("t", B.dep_r("wub%d" % (mt % 2), *XN[c]) + B.dep_w(pun), mmu, "T")
            B.did_w(mk, pun); B.did_r(mk, "wub%d" % (mt % 2), *XN[c])
            if mt + 2 < 64:
                dma_wtile(wu, mt + 2, wub, "wub", "DU")
            mk = B.emit("s", B.dep_r(pgn, "sgv") + B.dep_w("silg%d" % par),
                        lambda s, m_=mt, p_=pg, g_=silg[par]: s.activation(
                            g_[:], p_[:], AF.Silu, scale=sgv[:, m_:m_ + 1]), "S")
            B.did_r(mk, pgn, "sgv"); B.did_w(mk, "silg%d" % par)
            mk = B.emit("v", B.dep_r(pun, "suv", "silg%d" % par) + B.dep_w("fuse_t%d" % i2),
                        lambda v, m_=mt, i_=i2, p_=pu, g_=silg[par]: v.scalar_tensor_tensor(
                            fuse[:, i_ * RK:(i_ + 1) * RK], p_[:], suv[:, m_:m_ + 1], g_[:],
                            op0=ALU.mult, op1=ALU.mult), "V")
            B.did_r(mk, pun, "suv", "silg%d" % par); B.did_w(mk, "fuse_t%d" % i2)

        def fd_unit(grp, ot, bd):
            idx = grp * 16 + ot
            wsl = idx % 3
            pd, pdn = P[bd], "P%d" % bd
            fdeps = [B.dep_r("fuse_t%d" % i)[0] for i in range(16)]
            deps = B.dep_r("wdb%d" % wsl) + fdeps + B.dep_w(pdn)

            def mmd(t, p_=pd, w_=wdb[wsl]):
                r = None
                for i in range(16):
                    r = t.matmul(p_[:], w_[:, i * 128:(i + 1) * 128],
                                 fuse[:, i * RK:(i + 1) * RK],
                                 start=(i == 0), stop=(i == 15))
                return r
            mk = B.emit("t", deps, mmd, "T")
            B.did_w(mk, pdn)
            B.did_r(mk, "wdb%d" % wsl, *["fuse_t%d" % i for i in range(16)])
            if idx + 3 < 64:
                dma_wd_piece(idx + 3)
            if grp == 0:
                mk = B.emit("s", B.dep_r(pdn) + B.dep_w("acc_t%d" % ot),
                            lambda s, o_=ot, p_=pd: s.copy(acc[:, o_ * RK:(o_ + 1) * RK], p_[:]), "S")
            else:
                mk = B.emit("v", B.dep_r(pdn, "acc_t%d" % ot),
                            lambda v, o_=ot, p_=pd: v.tensor_tensor(
                                acc[:, o_ * RK:(o_ + 1) * RK], acc[:, o_ * RK:(o_ + 1) * RK],
                                p_[:], op=ALU.add), "V")
            B.did_r(mk, pdn); B.did_w(mk, "acc_t%d" % ot)

        def epilogue(c):
            outcol = c * RK
            for ot in range(NT):
                par = ot % 2
                mk = B.emit("q", B.dep_w("ard%d" % par) + B.dep_r("ascr%d_t%d" % (c, ot)),
                            lambda q, o_=ot, c_=c, a_=a_rd[par]: q.dma_start(
                                out=a_[:], in_=ascr[o_ * 128:(o_ + 1) * 128, c_ * RK:(c_ + 1) * RK]), "DB%d" % par)
                B.did_w(mk, "ard%d" % par); B.did_r(mk, "ascr%d_t%d" % (c, ot))
                mk = B.emit("g", B.dep_w("xrd%d" % par),
                            lambda g, o_=ot, x_=x_rd[par]: g.dma_start(
                                out=x_[:], in_=x[o_ * 128:(o_ + 1) * 128, outcol:outcol + RK]),
                            "DB%d" % par)
                B.did_w(mk, "xrd%d" % par)
                mk = B.emit("v", B.dep_r("ard%d" % par, "xrd%d" % par) + B.dep_w("axr%d" % par),
                            lambda v, a_=a_rd[par], x_=x_rd[par], r_=axr[par]: v.tensor_tensor(
                                r_[:], a_[:], x_[:], op=ALU.add), "V")
                B.did_r(mk, "ard%d" % par, "xrd%d" % par); B.did_w(mk, "axr%d" % par)
                mk = B.emit("v", B.dep_r("acc_t%d" % ot, "sdv", "axr%d" % par) + B.dep_w("ost%d" % par),
                            lambda v, o_=ot, r_=axr[par], u_=outst[par]: v.scalar_tensor_tensor(
                                u_[:], acc[:, o_ * RK:(o_ + 1) * RK], sdv[:, o_:o_ + 1], r_[:],
                                op0=ALU.mult, op1=ALU.add), "V")
                B.did_r(mk, "acc_t%d" % ot, "sdv", "axr%d" % par); B.did_w(mk, "ost%d" % par)
                mk = B.emit("q", B.dep_r("ost%d" % par),
                            lambda q, o_=ot, u_=outst[par]: q.dma_start(
                                out=out[o_ * 128:(o_ + 1) * 128, outcol:outcol + RK], in_=u_[:]), "DO")
                B.did_r(mk, "ost%d" % par)

        def q_dft_phase():
            steps = [(mt, c) for mt in range(NT) for c in (0, 1)]
            DELAY = 3

            def qproj_mm(i):
                mt, c = steps[i]
                wsl = mt % 4
                pb, pbn = P[3 + c], "P%d" % (3 + c)
                j = i % 4
                jr = i % 2
                qn = "qr%d" % j

                def mm(t, p_=pb, w_=wbuf[wsl], c_=c):
                    r = None
                    for kt in range(NT):
                        r = t.matmul(p_[:], w_[:, kt * 128:(kt + 1) * 128],
                                     xnb[c_][:, kt * RK:(kt + 1) * RK],
                                     start=(kt == 0), stop=(kt == NT - 1))
                    return r
                mk = B.emit("t", B.dep_r("wb%d" % wsl, *XN[c]) + B.dep_w(pbn), mm, "T")
                B.did_w(mk, pbn); B.did_r(mk, "wb%d" % wsl, *XN[c])
                mk = B.emit("s", B.dep_r(pbn) + B.dep_w(qn),
                            lambda s_, j_=j, p_=pb: s_.copy(qr[:, j_ * RK:(j_ + 1) * RK], p_[:]), "S")
                B.did_r(mk, pbn); B.did_w(mk, qn)
                mk = B.emit("s", B.dep_r(qn) + B.dep_w("sq%d" % jr),
                            lambda s_, j_=j, r_=jr: s_.square(
                                sqr[:, r_ * RK:(r_ + 1) * RK], qr[:, j_ * RK:(j_ + 1) * RK]), "S")
                B.did_w(mk, "sq%d" % jr); B.did_r(mk, qn)

            def qproj_norm(i):
                mt, c = steps[i]
                j = i % 4
                jr = i % 2
                qn = "qr%d" % j
                mk = B.emit("t", B.dep_r("sq%d" % jr, "onesF") + B.dep_w("P5"),
                            lambda t, r_=jr: t.matmul(
                                P[5][:], onesF[:], sqr[:, r_ * RK:(r_ + 1) * RK],
                                start=True, stop=True), "T")
                B.did_w(mk, "P5"); B.did_r(mk, "sq%d" % jr, "onesF")
                mk = B.emit("s", B.dep_r("P5") + B.dep_w("rcp%d" % jr),
                            lambda s_, r_=jr: s_.activation(
                                rcp[:, r_ * RK:(r_ + 1) * RK], P[5][:], AF.Sqrt, scale=C_Q), "S")
                B.did_w(mk, "rcp%d" % jr); B.did_r(mk, "P5")
                mk = B.emit("v", B.dep_w("rcp%d" % jr),
                            lambda v, r_=jr: v.reciprocal(
                                rcp[:, r_ * RK:(r_ + 1) * RK], rcp[:, r_ * RK:(r_ + 1) * RK]), "V")
                B.did_w(mk, "rcp%d" % jr)
                mk = B.emit("v", B.dep_w(qn) + B.dep_r("rcp%d" % jr),
                            lambda v, j_=j, r_=jr: v.tensor_tensor(
                                qr[:, j_ * RK:(j_ + 1) * RK], qr[:, j_ * RK:(j_ + 1) * RK],
                                rcp[:, r_ * RK:(r_ + 1) * RK], op=ALU.mult), "V")
                B.did_w(mk, qn); B.did_r(mk, "rcp%d" % jr)

            def dft(i):
                mt, c = steps[i]
                pb, pbn = P[3 + c], "P%d" % (3 + c)
                j = i % 4
                hp = i % 2
                qn = "qr%d" % j
                kvn = "kv%d_t%d" % (c, mt)
                names = [f"hb{hp}_{k}" for k in range(6)]
                qsl = qr[:, j * RK:(j + 1) * RK]
                ksl = kvb[c][:, mt * RK:(mt + 1) * RK]
                for (src_, srcn, mat, matn, dsti, pp, pn) in (
                        (qsl, qn, cfb, "cfb", 0, P[6], "P6"),
                        (qsl, qn, sfb, "sfb", 1, P[7], "P7"),
                        (ksl, kvn, cfb, "cfb", 2, pb, pbn),
                        (ksl, kvn, sfb, "sfb", 3, P[5], "P5")):
                    mk = B.emit("t", B.dep_r(srcn, matn) + B.dep_w(pn),
                                lambda t, s_=src_, w_=mat, p_=pp: t.matmul(
                                    p_[:], w_[:], s_, start=True, stop=True), "T")
                    B.did_w(mk, pn); B.did_r(mk, srcn, matn)
                    if dsti == 0:
                        mk = B.emit("v", B.dep_r(pn) + B.dep_w(names[dsti]),
                                    lambda v, b_=hb[hp][dsti], p_=pp: v.tensor_copy(b_[:], p_[:]), "V")
                    elif dsti == 2:
                        mk = B.emit("v", B.dep_r(pn) + B.dep_w(names[dsti]),
                                    lambda v, b_=hb[hp][dsti], p_=pp: v.tensor_copy(b_[:], p_[:]), "V")
                    else:
                        mk = B.emit("s", B.dep_r(pn) + B.dep_w(names[dsti]),
                                    lambda s_, b_=hb[hp][dsti], p_=pp: s_.copy(b_[:], p_[:]), "S")
                    B.did_r(mk, pn); B.did_w(mk, names[dsti])

                def prods(v, b_=hb[hp]):
                    QC, QS, KC, KS, T5, T6 = b_
                    v.tensor_tensor(T5[:], QC[:], KC[:], op=ALU.mult)
                    v.tensor_tensor(T6[:], QS[:], KS[:], op=ALU.mult)
                    v.tensor_tensor(T5[:], T5[:], T6[:], op=ALU.subtract)
                    v.tensor_tensor(T6[:], QC[:], KS[:], op=ALU.mult)
                    v.tensor_tensor(QC[:], QS[:], KC[:], op=ALU.mult)
                    return v.tensor_tensor(T6[:], T6[:], QC[:], op=ALU.add)
                mk = B.emit("v", B.dep_r(*names) + B.dep_w(names[0], names[4], names[5]), prods, "V")
                B.did_r(mk, *names); B.did_w(mk, names[0], names[4], names[5])

            def dft_inv(i):
                mt, c = steps[i]
                hp = i % 2
                kvn = "kv%d_t%d" % (c, mt)
                names = [f"hb{hp}_{k}" for k in range(6)]
                mk = B.emit("t", B.dep_r(names[4], names[5], "cfb", "sfb") + B.dep_w("P6"),
                            lambda t, b_=hb[hp]: (
                                t.matmul(P[6][:], cfb[:], b_[4][:], start=True, stop=False),
                                t.matmul(P[6][:], sfb[:], b_[5][:], start=False, stop=True))[-1], "T")
                B.did_w(mk, "P6"); B.did_r(mk, names[4], names[5], "cfb", "sfb")
                vhn = "vh%d_t%d" % (c, mt)
                mk = B.emit("s", B.dep_r("P6") + B.dep_w(kvn, vhn),
                            lambda s_, m_=mt, c_=c: s_.copy(
                                kvb[c_][:, m_ * RK:(m_ + 1) * RK], P[6][:]), "S")
                B.did_r(mk, "P6"); B.did_w(mk, kvn); B.did_w(mk, vhn)

            for i in range(len(steps)):
                qproj_mm(i)
                mt, c = steps[i]
                if c == 1 and mt + 4 < NT:
                    dma_wtile(wq, mt + 4, wbuf, "wb", "DW", nring=4)
                if i == 2:
                    carry_combine()
                    for k in range(3):
                        scan_step(k, steps)
                elif i >= 3:
                    scan_step(i, steps)
                if i >= 1:
                    qproj_norm(i - 1)
                if i >= DELAY:
                    dft(i - DELAY)
                if i >= DELAY + 1:
                    dft_inv(i - DELAY - 1)
                if i < 16:
                    fd_unit(0, i, 2)
                else:
                    fg_unit(0, 16 + (i - 16), 0, 1)
            qproj_norm(len(steps) - 1)
            for i in range(len(steps) - DELAY, len(steps)):
                dft(i)
                dft_inv(i - 1)
            dft_inv(len(steps) - 1)

        def wo_unit(o):
            ot, c = o // 2, o % 2
            wsl = ot % 4
            pb, pbn = P[3 + c], "P%d" % (3 + c)
            vh_names = ["vh%d_t%d" % (c, k) for k in range(NT)]

            def mmo(t, p_=pb, w_=wbuf[wsl], c_=c):
                r = None
                for kt in range(NT):
                    r = t.matmul(p_[:], w_[:, kt * 128:(kt + 1) * 128],
                                 kvb[c_][:, kt * RK:(kt + 1) * RK],
                                 start=(kt == 0), stop=(kt == NT - 1))
                return r
            mk = B.emit("t", B.dep_r("wb%d" % wsl, *vh_names) + B.dep_w(pbn), mmo, "T")
            B.did_w(mk, pbn); B.did_r(mk, "wb%d" % wsl, *vh_names)
            an = "ast%d" % c
            mk = B.emit("s", B.dep_r(pbn) + B.dep_w(an),
                        lambda s_, p_=pb, a_=ast[c]: s_.copy(a_[:], p_[:]), "S")
            B.did_r(mk, pbn); B.did_w(mk, an)
            mk = B.emit("q", B.dep_r(an) + B.dep_w("ascr%d_t%d" % (c, ot)),
                        lambda q, o_=ot, c_=c, a_=ast[c]: q.dma_start(
                            out=ascr[o_ * 128:(o_ + 1) * 128, c_ * RK:(c_ + 1) * RK], in_=a_[:]),
                        "DA%d" % c)
            B.did_r(mk, an); B.did_w(mk, "ascr%d_t%d" % (c, ot))
            if c == 1 and ot + 4 < NT:
                dma_wtile(wo, ot + 4, wbuf, "wb", "DW", nring=4)

        ln_chunk(0)
        ln_chunk(1)
        for mt in range(4):
            dma_wtile(wkv, mt, wbuf, "wb", "DW", nring=4)
        dma_wtile(wg, 0, wgb, "wgb", "DG")
        dma_wtile(wu, 0, wub, "wub", "DU")
        dma_wtile(wg, 1, wgb, "wgb", "DG")
        dma_wtile(wu, 1, wub, "wub", "DU")
        for k in range(NT):
            kv_mm_unit(k)
            fg_unit(0, k, 0, 1)
            if k >= 1:
                kv_norm_unit(k - 1)
        kv_norm_unit(NT - 1)
        for mt in range(4):
            dma_wtile(wq, mt, wbuf, "wb", "DW", nring=4)
        dma_wd_piece(0)
        dma_wd_piece(1)
        dma_wd_piece(2)
        carry_exchange()
        q_dft_phase()
        for ot in range(4):
            dma_wtile(wo, ot, wbuf, "wb", "DW", nring=4)
        for o in range(32):
            wo_unit(o)
            if o < 16:
                fd_unit(1, o, 2)
            else:
                fg_unit(0, 32 + (o - 16), 0, 1)
        for ot in range(NT):
            fd_unit(2, ot, 4 + ot % 2)
        for j2 in range(NT):
            fg_unit(0, 48 + j2, j2 % 2, 2 + j2 % 2)
        for ot in range(NT):
            fd_unit(3, ot, 4 + ot % 2)
        epilogue(0)
        dma_wtile(wg, 0, wgb, "wgb", "DG")
        dma_wtile(wu, 0, wub, "wub", "DU")
        dma_wtile(wg, 1, wgb, "wgb", "DG")
        dma_wtile(wu, 1, wub, "wub", "DU")
        for grp in range(4):
            for i2 in range(16):
                mt = grp * 16 + i2
                fg_unit(1, mt, mt % 2, 2 + mt % 2)
            if grp == 0:
                for k3 in range(3):
                    dma_wd_piece(k3)
            for ot in range(NT):
                fd_unit(grp, ot, 4 + ot % 2)
        epilogue(1)
        B.emit("q", [("DO", B.cnt["DO"])], lambda q: q.nop(), "G")
        B.flush()
    return nc


_NC_CACHE = None


def kernel(L0_attn_x, L0_attn_W_q, L0_attn_W_kv, L0_attn_W_o,
           L0_ffn_w_gate, L0_ffn_w_up, L0_ffn_w_down,
           L0_ffn_s_gate, L0_ffn_s_up, L0_ffn_s_down):
    x = np.asarray(L0_attn_x, dtype=np.float32)
    Wq = np.asarray(L0_attn_W_q, dtype=np.float32)
    Wkv = np.asarray(L0_attn_W_kv, dtype=np.float32)
    Wo = np.asarray(L0_attn_W_o, dtype=np.float32)
    wg = np.asarray(L0_ffn_w_gate, dtype=np.int8)
    wu = np.asarray(L0_ffn_w_up, dtype=np.int8)
    wd = np.asarray(L0_ffn_w_down, dtype=np.int8)
    sg = np.asarray(L0_ffn_s_gate, dtype=np.float32)
    su = np.asarray(L0_ffn_s_up, dtype=np.float32)
    sdn = np.asarray(L0_ffn_s_down, dtype=np.float32)

    t4a = lambda W: np.ascontiguousarray(W.reshape(16, 128, 16, 128).transpose(0, 3, 2, 1))
    wq_t, wkv_t, wo_t = t4a(Wq), t4a(Wkv), t4a(Wo)
    wg_t = np.ascontiguousarray(wg.reshape(16, 128, 64, 128).transpose(2, 1, 0, 3))
    wu_t = np.ascontiguousarray(wu.reshape(16, 128, 64, 128).transpose(2, 1, 0, 3))
    wd_t = np.ascontiguousarray(wd.reshape(64, 128, 16, 128).transpose(2, 1, 0, 3))
    sg_t = np.ascontiguousarray(sg.reshape(64, 128).T)
    su_t = np.ascontiguousarray(su.reshape(64, 128).T)
    sd_t = np.ascontiguousarray(sdn.reshape(16, 128).T)
    n = np.arange(128)
    ang = 2.0 * np.pi * np.outer(n, n) / 128.0
    cf = np.cos(ang).astype(np.float32)
    sf = np.sin(ang).astype(np.float32)

    in_maps = []
    for c in range(8):
        b, j = c // 4, c % 4
        x_c = np.ascontiguousarray(x[b, j * 1024:(j + 1) * 1024, :].T)
        mrow = np.zeros((8,), dtype=np.float32)
        for r in range(8):
            if r // 4 == b and r % 4 < j:
                mrow[r] = 1.0
        mcb = np.ascontiguousarray(np.tile(mrow, (128, 1)))
        in_maps.append({
            "x": x_c, "wq": wq_t, "wkv": wkv_t, "wo": wo_t,
            "wg": wg_t, "wu": wu_t, "wd": wd_t,
            "sg": sg_t, "su": su_t, "sd": sd_t,
            "cf": cf, "sf": sf, "mc": mcb,
        })

    global _NC_CACHE
    if _NC_CACHE is None:
        _NC_CACHE = build_program()
    res = run_bass_kernel_spmd(_NC_CACHE, in_maps, core_ids=list(range(8)))
    outp = np.empty((2, 4096, D), dtype=np.float32)
    for c in range(8):
        b, j = c // 4, c % 4
        outp[b, j * 1024:(j + 1) * 1024, :] = np.asarray(res.results[c]["out"]).T
    return outp


# version 16
# speedup vs baseline: 1.4478x; 1.0076x over previous
import contextlib
import numpy as np
import concourse.bass as bass
import concourse.mybir as mybir
from concourse.bass_utils import run_bass_kernel_spmd

F32 = mybir.dt.float32
BF16 = mybir.dt.bfloat16
I8 = mybir.dt.int8
AF = mybir.ActivationFunctionType
ALU = mybir.AluOpType

D = 2048
NT = 16
RK = 512
ROWS = 1024
C_Q = float(128 ** 3)
C_KV = 128.0
EPS = 1e-5

_EK = ("t", "v", "s", "g", "q")


class Bld:
    def __init__(self, nc, block, semnames, es):
        self.nc = nc
        self.sec = {"t": block.tensor, "v": block.vector, "s": block.scalar,
                    "g": block.gpsimd, "q": block.sync}
        self.sem, self.cnt = {}, {}
        self.seen = {k: {} for k in _EK}
        for n in semnames:
            self.sem[n] = es.enter_context(nc.semaphore(n))
            self.cnt[n] = 0
        self.wr, self.rd = {}, {}
        self.pend = {}

    def emit(self, ekey, deps, fn, inc):
        step = 16 if inc.startswith("D") else 1
        todo, seen = [], self.seen[ekey]
        for d in deps:
            if d is None:
                continue
            s, v = d
            if v > 0 and seen.get(s, 0) < v:
                todo.append((s, v))
                seen[s] = v
        semh, inc_sem = self.sem, self.sem[inc]

        def body(eng):
            for s, v in todo:
                eng.wait_ge(semh[s], v)
            fn(eng).then_inc(inc_sem, step)

        self.pend.setdefault(ekey, []).append(body)
        self.cnt[inc] += step
        return (inc, self.cnt[inc])

    def flush(self):
        for ekey, bodies in self.pend.items():
            def run_all(eng, bodies_=bodies):
                for b in bodies_:
                    b(eng)
            self.sec[ekey](run_all)
        self.pend = {}

    def dep_r(self, *bufs):
        return [self.wr.get(b) for b in bufs]

    def dep_w(self, *bufs):
        out = []
        for b in bufs:
            out.extend(self.rd.get(b, {}).items())
            if b in self.wr:
                out.append(self.wr[b])
        return out

    def did_r(self, mark, *bufs):
        for b in bufs:
            d = self.rd.setdefault(b, {})
            if d.get(mark[0], 0) < mark[1]:
                d[mark[0]] = mark[1]

    def did_w(self, mark, *bufs):
        for b in bufs:
            self.wr[b] = mark
            self.rd.pop(b, None)


def build_program():
    nc = bass.Bass("TRN2", target_bir_lowering=False)
    dti = lambda n, s, d: nc.dram_tensor(n, s, d, kind="ExternalInput").ap()
    x = dti("x", [D, ROWS], F32)
    wq = dti("wq", [16, 128, 16, 128], F32)
    wkv = dti("wkv", [16, 128, 16, 128], F32)
    wo = dti("wo", [16, 128, 16, 128], F32)
    wg = dti("wg", [64, 128, 16, 128], I8)
    wu = dti("wu", [64, 128, 16, 128], I8)
    wd = dti("wd", [16, 128, 64, 128], I8)
    sg = dti("sg", [128, 64], F32)
    su = dti("su", [128, 64], F32)
    sd = dti("sd", [128, 16], F32)
    cfm = dti("cf", [128, 128], F32)
    sfm = dti("sf", [128, 128], F32)
    mc = dti("mc", [128, 8], F32)
    out = nc.dram_tensor("out", [D, ROWS], F32, kind="ExternalOutput").ap()
    ascr = nc.dram_tensor("ascr", [D, ROWS], BF16, kind="Internal").ap()
    cc_in = nc.dram_tensor("cc_in", [128, 16], F32, kind="Internal")
    cc_out = nc.dram_tensor("cc_out", [8 * 128, 16], F32, kind="Internal")

    sems = ["T", "V", "S", "G", "CC", "DC",
            "DX0", "DX1", "DW0", "DW1", "DW2", "DW3", "DG0", "DG1", "DU0", "DU1",
            "DD0", "DD1", "DD2", "DA0", "DA1", "DB0", "DB1",
            "DI", "DH", "DO"]

    es = contextlib.ExitStack()
    with es:
        sb = lambda n, s, d: es.enter_context(nc.sbuf_tensor(n, s, d))
        xnb = [sb("xn0", [128, NT * RK], BF16), sb("xn1", [128, NT * RK], BF16)]
        kvb = [sb("kv0", [128, NT * RK], BF16), sb("kv1", [128, NT * RK], BF16)]
        qr = sb("qr", [128, 4 * RK], BF16)
        sqr = sb("sqr", [128, 4 * RK], BF16)
        rcp = sb("rcp", [128, 2 * RK], F32)
        mbuf = [sb("mb0", [128, RK], F32), sb("mb1", [128, RK], F32)]
        rsdb = [sb("rsd0", [128, RK], F32), sb("rsd1", [128, RK], F32)]
        wdb = [sb("wdb%d" % i, [128, 2048], BF16) for i in range(3)]
        hb = [[sb(f"hb{p}_{i}", [128, RK], BF16) for i in range(6)] for p in range(2)]
        wbuf = [sb("wb%d" % i, [128, 2048], BF16) for i in range(4)]
        wgb = [sb("wgb0", [128, 2048], BF16), sb("wgb1", [128, 2048], BF16)]
        wub = [sb("wub0", [128, 2048], BF16), sb("wub1", [128, 2048], BF16)]
        fuse = sb("fuse", [128, 16 * RK], BF16)
        silg = [sb("silg0", [128, RK], BF16), sb("silg1", [128, RK], BF16)]
        acc = sb("acc", [128, NT * RK], F32)
        ast = [sb("ast0", [128, RK], BF16), sb("ast1", [128, RK], BF16)]
        a_rd = [sb("ard0", [128, RK], BF16), sb("ard1", [128, RK], BF16)]
        x_rd = [sb("xrd0", [128, RK], BF16), sb("xrd1", [128, RK], BF16)]
        axr = [sb("axr0", [128, RK], F32), sb("axr1", [128, RK], F32)]
        outst = [sb("ost0", [128, RK], F32), sb("ost1", [128, RK], F32)]
        tots = [sb("tot0", [128, 16], F32), sb("tot1", [128, 16], F32)]
        ctot = sb("ctot", [128, 16], F32)
        carry = [sb("carA", [128, 16], F32), sb("car1", [128, 16], F32)]
        gth = sb("gth", [128, 128], F32)
        sgv = sb("sgv", [128, 64], F32)
        suv = sb("suv", [128, 64], F32)
        sdv = sb("sdv", [128, 16], F32)
        cfb = sb("cfb", [128, 128], BF16)
        sfb = sb("sfb", [128, 128], BF16)
        mcb = sb("mcb", [128, 8], F32)
        onesF = sb("onesF", [128, 128], BF16)
        onesS = sb("onesS", [128, RK], BF16)
        P = [es.enter_context(nc.psum_tensor(f"P{i}", [128, 512], F32)) for i in range(8)]
        print('SBUF remaining after alloc:', nc.sbuf_bytes_remaining)
        block = es.enter_context(nc.Block())
        B = Bld(nc, block, sems, es)


        def prolog(g):
            g.memset(onesF[:], 1.0)
            g.memset(onesS[:], 1.0)
            return g.memset(carry[0][:], 0.0)
        mk = B.emit("g", [], prolog, "G")
        B.did_w(mk, "onesF", "onesS", "carA")
        for dst, src_, nm in ((sgv, sg, "sgv"), (suv, su, "suv"), (sdv, sd, "sdv"),
                              (cfb, cfm, "cfb"), (sfb, sfm, "sfb"), (mcb, mc, "mcb")):
            mk = B.emit("g", [], (lambda d_, s_: (lambda g: g.dma_start(out=d_[:], in_=s_)))(dst, src_), "DC")
            B.did_w(mk, nm)
        for c in (0, 1):
            mk = B.emit("g", [],
                        lambda g, c_=c: g.dma_start(
                            out=xnb[c_][:, 0:NT * RK].rearrange("p (t r) -> p t r", t=NT),
                            in_=x[:, c_ * RK:(c_ + 1) * RK].rearrange("(t p) r -> p t r", p=128)),
                        "DX%d" % c)
            B.did_w(mk, *["xn%d_t%d" % (c, t) for t in range(NT)])

        XN = [["xn%d_t%d" % (c, t) for t in range(NT)] for c in (0, 1)]

        def dma_wtile(wsrc, mt, bufs, bname, sem_base, nring=2):
            sl = mt % nring
            nm = bname + str(sl)
            mk = B.emit("g", B.dep_w(nm),
                        lambda g, w_=wsrc, m_=mt, d_=bufs[sl]: g.dma_start(
                            out=d_[:], in_=w_[m_].rearrange("p a b -> p (a b)")),
                        "%s%d" % (sem_base, sl))
            B.did_w(mk, nm)

        def dma_wd_piece(idx):
            ot, grp, sl = idx % 16, idx // 16, idx % 3
            nm = "wdb%d" % sl
            mk = B.emit("g", B.dep_w(nm),
                        lambda g, o_=ot, p_=grp, d_=wdb[sl]: g.dma_start(
                            out=d_[:], in_=wd[o_, :, p_ * 16:(p_ + 1) * 16, :].rearrange("p a b -> p (a b)")),
                        "DD%d" % sl)
            B.did_w(mk, nm)

        def ln_chunk(c):
            xname = XN[c]
            ps, pq = 6, 7
            psn, pqn = "P%d" % ps, "P%d" % pq

            def mm_sum(t, c_=c):
                r = None
                for kt in range(NT):
                    r = t.matmul(P[ps][:], onesF[:], xnb[c_][:, kt * RK:(kt + 1) * RK],
                                 start=(kt == 0), stop=(kt == NT - 1))
                return r
            mk = B.emit("t", B.dep_r("onesF", *xname) + B.dep_w(psn), mm_sum, "T")
            B.did_w(mk, psn); B.did_r(mk, "onesF", *xname)
            for kt in range(NT):
                p = kt % 2
                mk = B.emit("s", B.dep_r(xname[kt]) + B.dep_w("sq%d" % p),
                            lambda s_, k_=kt, p_=p, c_=c: s_.square(
                                sqr[:, p_ * RK:(p_ + 1) * RK],
                                xnb[c_][:, k_ * RK:(k_ + 1) * RK]), "S")
                B.did_w(mk, "sq%d" % p); B.did_r(mk, xname[kt])
                mk2 = B.emit("t", B.dep_r("sq%d" % p, "onesF") + (B.dep_w(pqn) if kt == 0 else []),
                             lambda t, k_=kt, p_=p: t.matmul(P[pq][:], onesF[:], sqr[:, p_ * RK:(p_ + 1) * RK],
                                                             start=(k_ == 0), stop=(k_ == NT - 1)), "T")
                B.did_r(mk2, "sq%d" % p, "onesF")
                if kt == NT - 1:
                    B.did_w(mk2, pqn)
            mk = B.emit("v", B.dep_r(psn) + B.dep_w("mb%d" % c),
                        lambda v, c_=c: v.tensor_scalar_mul(mbuf[c_][:], P[ps][:], 1.0 / D), "V")
            B.did_w(mk, "mb%d" % c); B.did_r(mk, psn)
            mk = B.emit("v", B.dep_r("mb%d" % c) + B.dep_w("rsd%d" % c),
                        lambda v, c_=c: v.tensor_tensor(rsdb[c_][:], mbuf[c_][:], mbuf[c_][:], op=ALU.mult), "V")
            B.did_r(mk, "mb%d" % c)
            mk = B.emit("v", [mk] + B.dep_r(pqn),
                        lambda v, c_=c: v.scalar_tensor_tensor(
                            rsdb[c_][:], P[pq][:], 1.0 / D, rsdb[c_][:],
                            op0=ALU.mult, op1=ALU.subtract), "V")
            B.did_r(mk, pqn)
            mk = B.emit("v", [mk],
                        lambda v, c_=c: v.tensor_scalar_add(rsdb[c_][:], rsdb[c_][:], EPS), "V")
            B.did_w(mk, "rsd%d" % c)
            mk = B.emit("s", B.dep_w("rsd%d" % c),
                        lambda s, c_=c: s.activation(rsdb[c_][:], rsdb[c_][:], AF.Sqrt), "S")
            B.did_w(mk, "rsd%d" % c)
            mk = B.emit("v", B.dep_w("rsd%d" % c),
                        lambda v, c_=c: v.reciprocal(rsdb[c_][:], rsdb[c_][:]), "V")
            B.did_w(mk, "rsd%d" % c)
            ne, nsem = ("v", "V") if c == 0 else ("g", "G")
            for kt in range(NT):
                mk = B.emit(ne, B.dep_w(xname[kt]) + B.dep_r("mb%d" % c),
                            lambda v, k_=kt, c_=c: v.tensor_tensor(
                                xnb[c_][:, k_ * RK:(k_ + 1) * RK], xnb[c_][:, k_ * RK:(k_ + 1) * RK],
                                mbuf[c_][:], op=ALU.subtract), nsem)
                B.did_r(mk, "mb%d" % c)
                mk = B.emit(ne, [mk] + B.dep_r("rsd%d" % c),
                            lambda v, k_=kt, c_=c: v.tensor_tensor(
                                xnb[c_][:, k_ * RK:(k_ + 1) * RK], xnb[c_][:, k_ * RK:(k_ + 1) * RK],
                                rsdb[c_][:], op=ALU.mult), nsem)
                B.did_w(mk, xname[kt]); B.did_r(mk, "rsd%d" % c)

        def kv_mm_unit(mt):
            wsl = mt % 4
            for c in (0, 1):
                pb, pbn = P[3 + c], "P%d" % (3 + c)
                kvn = "kv%d_t%d" % (c, mt)

                def mm(t, p_=pb, w_=wbuf[wsl], c_=c):
                    r = None
                    for kt in range(NT):
                        r = t.matmul(p_[:], w_[:, kt * 128:(kt + 1) * 128],
                                     xnb[c_][:, kt * RK:(kt + 1) * RK],
                                     start=(kt == 0), stop=(kt == NT - 1))
                    return r
                mk = B.emit("t", B.dep_r("wb%d" % wsl, *XN[c]) + B.dep_w(pbn), mm, "T")
                B.did_w(mk, pbn); B.did_r(mk, "wb%d" % wsl, *XN[c])
                mk = B.emit("s", B.dep_r(pbn) + B.dep_w(kvn),
                            lambda s_, m_=mt, p_=pb, c_=c: s_.copy(
                                kvb[c_][:, m_ * RK:(m_ + 1) * RK], p_[:]), "S")
                B.did_r(mk, pbn); B.did_w(mk, kvn)
                jq = 2 * (mt % 2) + c
                mk = B.emit("s", B.dep_r(kvn) + B.dep_w("sq%d" % jq),
                            lambda s_, m_=mt, j_=jq, c_=c: s_.square(
                                sqr[:, j_ * RK:(j_ + 1) * RK],
                                kvb[c_][:, m_ * RK:(m_ + 1) * RK]), "S")
                B.did_w(mk, "sq%d" % jq); B.did_r(mk, kvn)
            if mt + 4 < NT:
                dma_wtile(wkv, mt + 4, wbuf, "wb", "DW", nring=4)

        def kv_norm_unit(mt):
            for c in (0, 1):
                kvn = "kv%d_t%d" % (c, mt)
                j = 2 * (mt % 2) + c
                nb, nbn = (P[5], "P5") if c == 0 else (P[6], "P6")
                mk = B.emit("t", B.dep_r("sq%d" % j, "onesF") + B.dep_w(nbn),
                            lambda t, j_=j, n_=nb: t.matmul(
                                n_[:], onesF[:], sqr[:, j_ * RK:(j_ + 1) * RK],
                                start=True, stop=True), "T")
                B.did_w(mk, nbn); B.did_r(mk, "sq%d" % j, "onesF")
                mk = B.emit("s", B.dep_r(nbn) + B.dep_w("rcp%d" % c),
                            lambda s_, j_=c, n_=nb: s_.activation(
                                rcp[:, j_ * RK:(j_ + 1) * RK], n_[:], AF.Sqrt, scale=C_KV), "S")
                B.did_w(mk, "rcp%d" % c); B.did_r(mk, nbn)
                mk = B.emit("v", B.dep_w("rcp%d" % c),
                            lambda v, j_=c: v.reciprocal(
                                rcp[:, j_ * RK:(j_ + 1) * RK], rcp[:, j_ * RK:(j_ + 1) * RK]), "V")
                B.did_w(mk, "rcp%d" % c)
                mk = B.emit("v", B.dep_w(kvn) + B.dep_r("rcp%d" % c),
                            lambda v, m_=mt, j_=c, c_=c: v.tensor_tensor(
                                kvb[c_][:, m_ * RK:(m_ + 1) * RK], kvb[c_][:, m_ * RK:(m_ + 1) * RK],
                                rcp[:, j_ * RK:(j_ + 1) * RK], op=ALU.mult), "V")
                B.did_w(mk, kvn); B.did_r(mk, "rcp%d" % c)
                mk = B.emit("v", B.dep_r(kvn) + B.dep_w("tot%d" % c),
                            lambda v, m_=mt, c_=c: v.tensor_reduce(
                                tots[c_][:, m_:m_ + 1], kvb[c_][:, m_ * RK:(m_ + 1) * RK],
                                mybir.AxisListType.X, ALU.add), "V")
                B.did_r(mk, kvn)
                if mt == NT - 1:
                    B.did_w(mk, "tot%d" % c)
                else:
                    B.did_w(mk, "totpart%d" % c)

        def carry_exchange():
            mk = B.emit("v", B.dep_r("tot0", "tot1") + B.dep_w("ctot"),
                        lambda v: v.tensor_tensor(ctot[:], tots[0][:], tots[1][:], op=ALU.add), "V")
            B.did_w(mk, "ctot"); B.did_r(mk, "tot0", "tot1")
            mk = B.emit("q", B.dep_r("ctot") + B.dep_w("cc_in"),
                        lambda q: q.dma_start(out=cc_in[:, :], in_=ctot[:]), "DI")
            B.did_r(mk, "ctot"); B.did_w(mk, "cc_in")
            mk = B.emit("g", B.dep_r("cc_in") + B.dep_w("cc_out"),
                        lambda g: g.collective_compute(
                            "AllGather", ALU.bypass,
                            replica_groups=[list(range(8))],
                            ins=[cc_in.ap().opt()], outs=[cc_out.ap().opt()]), "CC")
            B.did_r(mk, "cc_in"); B.did_w(mk, "cc_out")
            mk = B.emit("q", B.dep_r("cc_out") + B.dep_w("gth"),
                        lambda q: q.dma_start(
                            out=gth[:].rearrange("p (r t) -> p r t", r=8),
                            in_=cc_out.ap().rearrange("(r p) t -> p r t", p=128)), "DH")
            B.did_w(mk, "gth"); B.did_r(mk, "cc_out")

        def carry_combine():
            for r in range(8):
                mk = B.emit("v", B.dep_r("gth", "mcb") + B.dep_w("carA"),
                            lambda v, r_=r: v.scalar_tensor_tensor(
                                carry[0][:], gth[:, r_ * 16:(r_ + 1) * 16], mcb[:, r_:r_ + 1],
                                carry[0][:], op0=ALU.mult, op1=ALU.add), "V")
                B.did_w(mk, "carA"); B.did_r(mk, "gth", "mcb")
            mk = B.emit("v", B.dep_r("carA", "tot0") + B.dep_w("car1"),
                        lambda v: v.tensor_tensor(carry[1][:], carry[0][:], tots[0][:], op=ALU.add), "V")
            B.did_w(mk, "car1"); B.did_r(mk, "carA", "tot0")

        def scan_step(i, steps):
            mt, c = steps[i]
            kvn = "kv%d_t%d" % (c, mt)
            cn = "carA" if c == 0 else "car1"
            mk = B.emit("v", B.dep_w(kvn) + B.dep_r(cn, "onesS"),
                        lambda v, m_=mt, c_=c: v.tensor_tensor_scan(
                            kvb[c_][:, m_ * RK:(m_ + 1) * RK], onesS[:],
                            kvb[c_][:, m_ * RK:(m_ + 1) * RK],
                            carry[c_][:, m_:m_ + 1],
                            op0=ALU.mult, op1=ALU.add), "V")
            B.did_w(mk, kvn); B.did_r(mk, cn, "onesS")

        def fg_unit(c, mt, bg, bu):
            i2 = mt % 16
            par = mt % 2
            pg, pgn = P[bg], "P%d" % bg
            pu, pun = P[bu], "P%d" % bu

            def mmg(t, p_=pg, w_=wgb[mt % 2], c_=c):
                r = None
                for kt in range(NT):
                    r = t.matmul(p_[:], w_[:, kt * 128:(kt + 1) * 128],
                                 xnb[c_][:, kt * RK:(kt + 1) * RK],
                                 start=(kt == 0), stop=(kt == NT - 1))
                return r
            mk = B.emit("t", B.dep_r("wgb%d" % (mt % 2), *XN[c]) + B.dep_w(pgn), mmg, "T")
            B.did_w(mk, pgn); B.did_r(mk, "wgb%d" % (mt % 2), *XN[c])
            if mt + 2 < 64:
                dma_wtile(wg, mt + 2, wgb, "wgb", "DG")

            def mmu(t, p_=pu, w_=wub[mt % 2], c_=c):
                r = None
                for kt in range(NT):
                    r = t.matmul(p_[:], w_[:, kt * 128:(kt + 1) * 128],
                                 xnb[c_][:, kt * RK:(kt + 1) * RK],
                                 start=(kt == 0), stop=(kt == NT - 1))
                return r
            mk = B.emit("t", B.dep_r("wub%d" % (mt % 2), *XN[c]) + B.dep_w(pun), mmu, "T")
            B.did_w(mk, pun); B.did_r(mk, "wub%d" % (mt % 2), *XN[c])
            if mt + 2 < 64:
                dma_wtile(wu, mt + 2, wub, "wub", "DU")
            mk = B.emit("s", B.dep_r(pgn, "sgv") + B.dep_w("silg%d" % par),
                        lambda s, m_=mt, p_=pg, g_=silg[par]: s.activation(
                            g_[:], p_[:], AF.Silu, scale=sgv[:, m_:m_ + 1]), "S")
            B.did_r(mk, pgn, "sgv"); B.did_w(mk, "silg%d" % par)
            mk = B.emit("v", B.dep_r(pun, "suv", "silg%d" % par) + B.dep_w("fuse_t%d" % i2),
                        lambda v, m_=mt, i_=i2, p_=pu, g_=silg[par]: v.scalar_tensor_tensor(
                            fuse[:, i_ * RK:(i_ + 1) * RK], p_[:], suv[:, m_:m_ + 1], g_[:],
                            op0=ALU.mult, op1=ALU.mult), "V")
            B.did_r(mk, pun, "suv", "silg%d" % par); B.did_w(mk, "fuse_t%d" % i2)

        def fd_unit(grp, ot, bd):
            idx = grp * 16 + ot
            wsl = idx % 3
            pd, pdn = P[bd], "P%d" % bd
            fdeps = [B.dep_r("fuse_t%d" % i)[0] for i in range(16)]
            deps = B.dep_r("wdb%d" % wsl) + fdeps + B.dep_w(pdn)

            def mmd(t, p_=pd, w_=wdb[wsl]):
                r = None
                for i in range(16):
                    r = t.matmul(p_[:], w_[:, i * 128:(i + 1) * 128],
                                 fuse[:, i * RK:(i + 1) * RK],
                                 start=(i == 0), stop=(i == 15))
                return r
            mk = B.emit("t", deps, mmd, "T")
            B.did_w(mk, pdn)
            B.did_r(mk, "wdb%d" % wsl, *["fuse_t%d" % i for i in range(16)])
            if idx + 3 < 64:
                dma_wd_piece(idx + 3)
            if grp == 0:
                mk = B.emit("s", B.dep_r(pdn) + B.dep_w("acc_t%d" % ot),
                            lambda s, o_=ot, p_=pd: s.copy(acc[:, o_ * RK:(o_ + 1) * RK], p_[:]), "S")
            else:
                mk = B.emit("v", B.dep_r(pdn, "acc_t%d" % ot),
                            lambda v, o_=ot, p_=pd: v.tensor_tensor(
                                acc[:, o_ * RK:(o_ + 1) * RK], acc[:, o_ * RK:(o_ + 1) * RK],
                                p_[:], op=ALU.add), "V")
            B.did_r(mk, pdn); B.did_w(mk, "acc_t%d" % ot)

        def epilogue(c):
            outcol = c * RK
            for ot in range(NT):
                par = ot % 2
                mk = B.emit("q", B.dep_w("ard%d" % par) + B.dep_r("ascr%d_t%d" % (c, ot)),
                            lambda q, o_=ot, c_=c, a_=a_rd[par]: q.dma_start(
                                out=a_[:], in_=ascr[o_ * 128:(o_ + 1) * 128, c_ * RK:(c_ + 1) * RK]), "DB%d" % par)
                B.did_w(mk, "ard%d" % par); B.did_r(mk, "ascr%d_t%d" % (c, ot))
                mk = B.emit("g", B.dep_w("xrd%d" % par),
                            lambda g, o_=ot, x_=x_rd[par]: g.dma_start(
                                out=x_[:], in_=x[o_ * 128:(o_ + 1) * 128, outcol:outcol + RK]),
                            "DB%d" % par)
                B.did_w(mk, "xrd%d" % par)
                mk = B.emit("v", B.dep_r("ard%d" % par, "xrd%d" % par) + B.dep_w("axr%d" % par),
                            lambda v, a_=a_rd[par], x_=x_rd[par], r_=axr[par]: v.tensor_tensor(
                                r_[:], a_[:], x_[:], op=ALU.add), "V")
                B.did_r(mk, "ard%d" % par, "xrd%d" % par); B.did_w(mk, "axr%d" % par)
                mk = B.emit("v", B.dep_r("acc_t%d" % ot, "sdv", "axr%d" % par) + B.dep_w("ost%d" % par),
                            lambda v, o_=ot, r_=axr[par], u_=outst[par]: v.scalar_tensor_tensor(
                                u_[:], acc[:, o_ * RK:(o_ + 1) * RK], sdv[:, o_:o_ + 1], r_[:],
                                op0=ALU.mult, op1=ALU.add), "V")
                B.did_r(mk, "acc_t%d" % ot, "sdv", "axr%d" % par); B.did_w(mk, "ost%d" % par)
                mk = B.emit("q", B.dep_r("ost%d" % par),
                            lambda q, o_=ot, u_=outst[par]: q.dma_start(
                                out=out[o_ * 128:(o_ + 1) * 128, outcol:outcol + RK], in_=u_[:]), "DO")
                B.did_r(mk, "ost%d" % par)

        def q_dft_phase():
            steps = [(mt, c) for mt in range(NT) for c in (0, 1)]
            DELAY = 3

            def qproj_mm(i):
                mt, c = steps[i]
                wsl = mt % 4
                pb, pbn = P[3 + c], "P%d" % (3 + c)
                j = i % 4
                jr = i % 2
                qn = "qr%d" % j

                def mm(t, p_=pb, w_=wbuf[wsl], c_=c):
                    r = None
                    for kt in range(NT):
                        r = t.matmul(p_[:], w_[:, kt * 128:(kt + 1) * 128],
                                     xnb[c_][:, kt * RK:(kt + 1) * RK],
                                     start=(kt == 0), stop=(kt == NT - 1))
                    return r
                mk = B.emit("t", B.dep_r("wb%d" % wsl, *XN[c]) + B.dep_w(pbn), mm, "T")
                B.did_w(mk, pbn); B.did_r(mk, "wb%d" % wsl, *XN[c])
                mk = B.emit("s", B.dep_r(pbn) + B.dep_w(qn),
                            lambda s_, j_=j, p_=pb: s_.copy(qr[:, j_ * RK:(j_ + 1) * RK], p_[:]), "S")
                B.did_r(mk, pbn); B.did_w(mk, qn)
                mk = B.emit("s", B.dep_r(qn) + B.dep_w("sq%d" % jr),
                            lambda s_, j_=j, r_=jr: s_.square(
                                sqr[:, r_ * RK:(r_ + 1) * RK], qr[:, j_ * RK:(j_ + 1) * RK]), "S")
                B.did_w(mk, "sq%d" % jr); B.did_r(mk, qn)

            def qproj_norm(i):
                mt, c = steps[i]
                j = i % 4
                jr = i % 2
                qn = "qr%d" % j
                mk = B.emit("t", B.dep_r("sq%d" % jr, "onesF") + B.dep_w("P5"),
                            lambda t, r_=jr: t.matmul(
                                P[5][:], onesF[:], sqr[:, r_ * RK:(r_ + 1) * RK],
                                start=True, stop=True), "T")
                B.did_w(mk, "P5"); B.did_r(mk, "sq%d" % jr, "onesF")
                mk = B.emit("s", B.dep_r("P5") + B.dep_w("rcp%d" % jr),
                            lambda s_, r_=jr: s_.activation(
                                rcp[:, r_ * RK:(r_ + 1) * RK], P[5][:], AF.Sqrt, scale=C_Q), "S")
                B.did_w(mk, "rcp%d" % jr); B.did_r(mk, "P5")
                mk = B.emit("v", B.dep_w("rcp%d" % jr),
                            lambda v, r_=jr: v.reciprocal(
                                rcp[:, r_ * RK:(r_ + 1) * RK], rcp[:, r_ * RK:(r_ + 1) * RK]), "V")
                B.did_w(mk, "rcp%d" % jr)
                mk = B.emit("v", B.dep_w(qn) + B.dep_r("rcp%d" % jr),
                            lambda v, j_=j, r_=jr: v.tensor_tensor(
                                qr[:, j_ * RK:(j_ + 1) * RK], qr[:, j_ * RK:(j_ + 1) * RK],
                                rcp[:, r_ * RK:(r_ + 1) * RK], op=ALU.mult), "V")
                B.did_w(mk, qn); B.did_r(mk, "rcp%d" % jr)

            def dft(i):
                mt, c = steps[i]
                pb, pbn = P[3 + c], "P%d" % (3 + c)
                j = i % 4
                hp = i % 2
                qn = "qr%d" % j
                kvn = "kv%d_t%d" % (c, mt)
                names = [f"hb{hp}_{k}" for k in range(6)]
                qsl = qr[:, j * RK:(j + 1) * RK]
                ksl = kvb[c][:, mt * RK:(mt + 1) * RK]
                for (src_, srcn, mat, matn, dsti, pp, pn) in (
                        (qsl, qn, cfb, "cfb", 0, P[6], "P6"),
                        (qsl, qn, sfb, "sfb", 1, P[7], "P7"),
                        (ksl, kvn, cfb, "cfb", 2, pb, pbn),
                        (ksl, kvn, sfb, "sfb", 3, P[5], "P5")):
                    mk = B.emit("t", B.dep_r(srcn, matn) + B.dep_w(pn),
                                lambda t, s_=src_, w_=mat, p_=pp: t.matmul(
                                    p_[:], w_[:], s_, start=True, stop=True), "T")
                    B.did_w(mk, pn); B.did_r(mk, srcn, matn)
                    if dsti == 0:
                        mk = B.emit("v", B.dep_r(pn) + B.dep_w(names[dsti]),
                                    lambda v, b_=hb[hp][dsti], p_=pp: v.tensor_copy(b_[:], p_[:]), "V")
                    elif dsti == 2:
                        mk = B.emit("v", B.dep_r(pn) + B.dep_w(names[dsti]),
                                    lambda v, b_=hb[hp][dsti], p_=pp: v.tensor_copy(b_[:], p_[:]), "V")
                    else:
                        mk = B.emit("s", B.dep_r(pn) + B.dep_w(names[dsti]),
                                    lambda s_, b_=hb[hp][dsti], p_=pp: s_.copy(b_[:], p_[:]), "S")
                    B.did_r(mk, pn); B.did_w(mk, names[dsti])

                def prods(v, b_=hb[hp]):
                    QC, QS, KC, KS, T5, T6 = b_
                    v.tensor_tensor(T5[:], QC[:], KC[:], op=ALU.mult)
                    v.tensor_tensor(T6[:], QS[:], KS[:], op=ALU.mult)
                    v.tensor_tensor(T5[:], T5[:], T6[:], op=ALU.subtract)
                    v.tensor_tensor(T6[:], QC[:], KS[:], op=ALU.mult)
                    v.tensor_tensor(QC[:], QS[:], KC[:], op=ALU.mult)
                    return v.tensor_tensor(T6[:], T6[:], QC[:], op=ALU.add)
                mk = B.emit("v", B.dep_r(*names) + B.dep_w(names[0], names[4], names[5]), prods, "V")
                B.did_r(mk, *names); B.did_w(mk, names[0], names[4], names[5])

            def dft_inv(i):
                mt, c = steps[i]
                hp = i % 2
                kvn = "kv%d_t%d" % (c, mt)
                names = [f"hb{hp}_{k}" for k in range(6)]
                mk = B.emit("t", B.dep_r(names[4], names[5], "cfb", "sfb") + B.dep_w("P6"),
                            lambda t, b_=hb[hp]: (
                                t.matmul(P[6][:], cfb[:], b_[4][:], start=True, stop=False),
                                t.matmul(P[6][:], sfb[:], b_[5][:], start=False, stop=True))[-1], "T")
                B.did_w(mk, "P6"); B.did_r(mk, names[4], names[5], "cfb", "sfb")
                vhn = "vh%d_t%d" % (c, mt)
                mk = B.emit("s", B.dep_r("P6") + B.dep_w(kvn, vhn),
                            lambda s_, m_=mt, c_=c: s_.copy(
                                kvb[c_][:, m_ * RK:(m_ + 1) * RK], P[6][:]), "S")
                B.did_r(mk, "P6"); B.did_w(mk, kvn); B.did_w(mk, vhn)

            for i in range(len(steps)):
                qproj_mm(i)
                mt, c = steps[i]
                if c == 1 and mt + 4 < NT:
                    dma_wtile(wq, mt + 4, wbuf, "wb", "DW", nring=4)
                if i == 2:
                    carry_combine()
                    for k in range(3):
                        scan_step(k, steps)
                elif i >= 3:
                    scan_step(i, steps)
                if i >= 1:
                    qproj_norm(i - 1)
                if i >= DELAY:
                    dft(i - DELAY)
                if i >= DELAY + 1:
                    dft_inv(i - DELAY - 1)
                if i < 16:
                    fd_unit(0, i, 2)
                else:
                    fg_unit(0, 16 + (i - 16), 0, 1)
            qproj_norm(len(steps) - 1)
            for i in range(len(steps) - DELAY, len(steps)):
                dft(i)
                dft_inv(i - 1)
            dft_inv(len(steps) - 1)

        def wo_unit(o):
            ot, c = o // 2, o % 2
            wsl = ot % 4
            pb, pbn = P[3 + c], "P%d" % (3 + c)
            vh_names = ["vh%d_t%d" % (c, k) for k in range(NT)]

            def mmo(t, p_=pb, w_=wbuf[wsl], c_=c):
                r = None
                for kt in range(NT):
                    r = t.matmul(p_[:], w_[:, kt * 128:(kt + 1) * 128],
                                 kvb[c_][:, kt * RK:(kt + 1) * RK],
                                 start=(kt == 0), stop=(kt == NT - 1))
                return r
            mk = B.emit("t", B.dep_r("wb%d" % wsl, *vh_names) + B.dep_w(pbn), mmo, "T")
            B.did_w(mk, pbn); B.did_r(mk, "wb%d" % wsl, *vh_names)
            an = "ast%d" % c
            mk = B.emit("s", B.dep_r(pbn) + B.dep_w(an),
                        lambda s_, p_=pb, a_=ast[c]: s_.copy(a_[:], p_[:]), "S")
            B.did_r(mk, pbn); B.did_w(mk, an)
            mk = B.emit("q", B.dep_r(an) + B.dep_w("ascr%d_t%d" % (c, ot)),
                        lambda q, o_=ot, c_=c, a_=ast[c]: q.dma_start(
                            out=ascr[o_ * 128:(o_ + 1) * 128, c_ * RK:(c_ + 1) * RK], in_=a_[:]),
                        "DA%d" % c)
            B.did_r(mk, an); B.did_w(mk, "ascr%d_t%d" % (c, ot))
            if c == 1 and ot + 4 < NT:
                dma_wtile(wo, ot + 4, wbuf, "wb", "DW", nring=4)

        ln_chunk(0)
        ln_chunk(1)
        for mt in range(4):
            dma_wtile(wkv, mt, wbuf, "wb", "DW", nring=4)
        dma_wtile(wg, 0, wgb, "wgb", "DG")
        dma_wtile(wu, 0, wub, "wub", "DU")
        dma_wtile(wg, 1, wgb, "wgb", "DG")
        dma_wtile(wu, 1, wub, "wub", "DU")
        for k in range(NT):
            kv_mm_unit(k)
            fg_unit(0, k, 0, 1)
            if k >= 1:
                kv_norm_unit(k - 1)
        kv_norm_unit(NT - 1)
        for mt in range(4):
            dma_wtile(wq, mt, wbuf, "wb", "DW", nring=4)
        dma_wd_piece(0)
        dma_wd_piece(1)
        dma_wd_piece(2)
        carry_exchange()
        q_dft_phase()
        for ot in range(4):
            dma_wtile(wo, ot, wbuf, "wb", "DW", nring=4)
        for o in range(32):
            wo_unit(o)
            if o < 16:
                fd_unit(1, o, 2)
            else:
                fg_unit(0, 32 + (o - 16), 0, 1)
        for ot in range(NT):
            fd_unit(2, ot, 4 + ot % 2)
        for j2 in range(NT):
            fg_unit(0, 48 + j2, j2 % 2, 2 + j2 % 2)
        for ot in range(NT):
            fd_unit(3, ot, 4 + ot % 2)
        epilogue(0)
        dma_wtile(wg, 0, wgb, "wgb", "DG")
        dma_wtile(wu, 0, wub, "wub", "DU")
        dma_wtile(wg, 1, wgb, "wgb", "DG")
        dma_wtile(wu, 1, wub, "wub", "DU")
        for grp in range(4):
            for i2 in range(16):
                mt = grp * 16 + i2
                fg_unit(1, mt, mt % 2, 2 + mt % 2)
            if grp == 0:
                for k3 in range(3):
                    dma_wd_piece(k3)
            for ot in range(NT):
                fd_unit(grp, ot, 4 + ot % 2)
        epilogue(1)
        B.emit("q", [("DO", B.cnt["DO"])], lambda q: q.nop(), "G")
        B.flush()
    return nc


_NC_CACHE = None


def kernel(L0_attn_x, L0_attn_W_q, L0_attn_W_kv, L0_attn_W_o,
           L0_ffn_w_gate, L0_ffn_w_up, L0_ffn_w_down,
           L0_ffn_s_gate, L0_ffn_s_up, L0_ffn_s_down):
    x = np.asarray(L0_attn_x, dtype=np.float32)
    Wq = np.asarray(L0_attn_W_q, dtype=np.float32)
    Wkv = np.asarray(L0_attn_W_kv, dtype=np.float32)
    Wo = np.asarray(L0_attn_W_o, dtype=np.float32)
    wg = np.asarray(L0_ffn_w_gate, dtype=np.int8)
    wu = np.asarray(L0_ffn_w_up, dtype=np.int8)
    wd = np.asarray(L0_ffn_w_down, dtype=np.int8)
    sg = np.asarray(L0_ffn_s_gate, dtype=np.float32)
    su = np.asarray(L0_ffn_s_up, dtype=np.float32)
    sdn = np.asarray(L0_ffn_s_down, dtype=np.float32)

    t4a = lambda W: np.ascontiguousarray(W.reshape(16, 128, 16, 128).transpose(0, 3, 2, 1))
    wq_t, wkv_t, wo_t = t4a(Wq), t4a(Wkv), t4a(Wo)
    wg_t = np.ascontiguousarray(wg.reshape(16, 128, 64, 128).transpose(2, 1, 0, 3))
    wu_t = np.ascontiguousarray(wu.reshape(16, 128, 64, 128).transpose(2, 1, 0, 3))
    wd_t = np.ascontiguousarray(wd.reshape(64, 128, 16, 128).transpose(2, 1, 0, 3))
    sg_t = np.ascontiguousarray(sg.reshape(64, 128).T)
    su_t = np.ascontiguousarray(su.reshape(64, 128).T)
    sd_t = np.ascontiguousarray(sdn.reshape(16, 128).T)
    n = np.arange(128)
    ang = 2.0 * np.pi * np.outer(n, n) / 128.0
    cf = np.cos(ang).astype(np.float32)
    sf = np.sin(ang).astype(np.float32)

    in_maps = []
    for c in range(8):
        b, j = c // 4, c % 4
        x_c = np.ascontiguousarray(x[b, j * 1024:(j + 1) * 1024, :].T)
        mrow = np.zeros((8,), dtype=np.float32)
        for r in range(8):
            if r // 4 == b and r % 4 < j:
                mrow[r] = 1.0
        mcb = np.ascontiguousarray(np.tile(mrow, (128, 1)))
        in_maps.append({
            "x": x_c, "wq": wq_t, "wkv": wkv_t, "wo": wo_t,
            "wg": wg_t, "wu": wu_t, "wd": wd_t,
            "sg": sg_t, "su": su_t, "sd": sd_t,
            "cf": cf, "sf": sf, "mc": mcb,
        })

    global _NC_CACHE
    if _NC_CACHE is None:
        _NC_CACHE = build_program()
    res = run_bass_kernel_spmd(_NC_CACHE, in_maps, core_ids=list(range(8)))
    outp = np.empty((2, 4096, D), dtype=np.float32)
    for c in range(8):
        b, j = c // 4, c % 4
        outp[b, j * 1024:(j + 1) * 1024, :] = np.asarray(res.results[c]["out"]).T
    return outp


# version 17
# speedup vs baseline: 1.4519x; 1.0029x over previous
import contextlib
import numpy as np
import concourse.bass as bass
import concourse.mybir as mybir
from concourse.bass_utils import run_bass_kernel_spmd

F32 = mybir.dt.float32
BF16 = mybir.dt.bfloat16
I8 = mybir.dt.int8
AF = mybir.ActivationFunctionType
ALU = mybir.AluOpType

D = 2048
NT = 16
RK = 512
ROWS = 1024
C_Q = float(128 ** 3)
C_KV = 128.0
EPS = 1e-5

_EK = ("t", "v", "s", "g", "q")


class Bld:
    def __init__(self, nc, block, semnames, es):
        self.nc = nc
        self.sec = {"t": block.tensor, "v": block.vector, "s": block.scalar,
                    "g": block.gpsimd, "q": block.sync}
        self.sem, self.cnt = {}, {}
        self.seen = {k: {} for k in _EK}
        for n in semnames:
            self.sem[n] = es.enter_context(nc.semaphore(n))
            self.cnt[n] = 0
        self.wr, self.rd = {}, {}
        self.pend = {}

    def emit(self, ekey, deps, fn, inc):
        step = 16 if inc.startswith("D") else 1
        todo, seen = [], self.seen[ekey]
        for d in deps:
            if d is None:
                continue
            s, v = d
            if v > 0 and seen.get(s, 0) < v:
                todo.append((s, v))
                seen[s] = v
        semh, inc_sem = self.sem, self.sem[inc]

        def body(eng):
            for s, v in todo:
                eng.wait_ge(semh[s], v)
            fn(eng).then_inc(inc_sem, step)

        self.pend.setdefault(ekey, []).append(body)
        self.cnt[inc] += step
        return (inc, self.cnt[inc])

    def flush(self):
        for ekey, bodies in self.pend.items():
            def run_all(eng, bodies_=bodies):
                for b in bodies_:
                    b(eng)
            self.sec[ekey](run_all)
        self.pend = {}

    def dep_r(self, *bufs):
        return [self.wr.get(b) for b in bufs]

    def dep_w(self, *bufs):
        out = []
        for b in bufs:
            out.extend(self.rd.get(b, {}).items())
            if b in self.wr:
                out.append(self.wr[b])
        return out

    def did_r(self, mark, *bufs):
        for b in bufs:
            d = self.rd.setdefault(b, {})
            if d.get(mark[0], 0) < mark[1]:
                d[mark[0]] = mark[1]

    def did_w(self, mark, *bufs):
        for b in bufs:
            self.wr[b] = mark
            self.rd.pop(b, None)


def build_program():
    nc = bass.Bass("TRN2", target_bir_lowering=False)
    dti = lambda n, s, d: nc.dram_tensor(n, s, d, kind="ExternalInput").ap()
    x = dti("x", [D, ROWS], F32)
    wq = dti("wq", [16, 128, 16, 128], F32)
    wkv = dti("wkv", [16, 128, 16, 128], F32)
    wo = dti("wo", [16, 128, 16, 128], F32)
    wg = dti("wg", [64, 128, 16, 128], I8)
    wu = dti("wu", [64, 128, 16, 128], I8)
    wd = dti("wd", [16, 128, 64, 128], I8)
    sg = dti("sg", [128, 64], F32)
    su = dti("su", [128, 64], F32)
    sd = dti("sd", [128, 16], F32)
    cfm = dti("cf", [128, 128], F32)
    sfm = dti("sf", [128, 128], F32)
    mc = dti("mc", [128, 8], F32)
    out = nc.dram_tensor("out", [D, ROWS], F32, kind="ExternalOutput").ap()
    ascr = nc.dram_tensor("ascr", [D, ROWS], BF16, kind="Internal").ap()
    cc_in = nc.dram_tensor("cc_in", [128, 16], F32, kind="Internal")
    cc_out = nc.dram_tensor("cc_out", [8 * 128, 16], F32, kind="Internal")

    sems = ["T", "V", "S", "G", "CC", "DC",
            "DX0", "DX1", "DW0", "DW1", "DW2", "DW3", "DG0", "DG1", "DU0", "DU1",
            "DD0", "DD1", "DD2", "DA0", "DA1", "DB0", "DB1", "DR0", "DR1",
            "DI", "DH", "DO"]

    es = contextlib.ExitStack()
    with es:
        sb = lambda n, s, d: es.enter_context(nc.sbuf_tensor(n, s, d))
        xnb = [sb("xn0", [128, NT * RK], BF16), sb("xn1", [128, NT * RK], BF16)]
        kvb = [sb("kv0", [128, NT * RK], BF16), sb("kv1", [128, NT * RK], BF16)]
        qr = sb("qr", [128, 4 * RK], BF16)
        sqr = sb("sqr", [128, 4 * RK], BF16)
        rcp = sb("rcp", [128, 2 * RK], F32)
        mbuf = [sb("mb0", [128, RK], F32), sb("mb1", [128, RK], F32)]
        rsdb = [sb("rsd0", [128, RK], F32), sb("rsd1", [128, RK], F32)]
        wdb = [sb("wdb%d" % i, [128, 2048], BF16) for i in range(3)]
        hb = [[sb(f"hb{p}_{i}", [128, RK], BF16) for i in range(6)] for p in range(2)]
        wbuf = [sb("wb%d" % i, [128, 2048], BF16) for i in range(4)]
        wgb = [sb("wgb0", [128, 2048], BF16), sb("wgb1", [128, 2048], BF16)]
        wub = [sb("wub0", [128, 2048], BF16), sb("wub1", [128, 2048], BF16)]
        fuse = sb("fuse", [128, 16 * RK], BF16)
        silg = [sb("silg0", [128, RK], BF16), sb("silg1", [128, RK], BF16)]
        acc = sb("acc", [128, NT * RK], F32)
        ast = [sb("ast0", [128, RK], BF16), sb("ast1", [128, RK], BF16)]
        a_rd = [sb("ard0", [128, RK], BF16), sb("ard1", [128, RK], BF16)]
        x_rd = [sb("xrd0", [128, RK], BF16), sb("xrd1", [128, RK], BF16)]
        axr = [sb("axr0", [128, RK], F32), sb("axr1", [128, RK], F32)]
        outst = [sb("ost0", [128, RK], F32), sb("ost1", [128, RK], F32)]
        tots = [sb("tot0", [128, 16], F32), sb("tot1", [128, 16], F32)]
        ctot = sb("ctot", [128, 16], F32)
        carry = [sb("carA", [128, 16], F32), sb("car1", [128, 16], F32)]
        gth = sb("gth", [128, 128], F32)
        sgv = sb("sgv", [128, 64], F32)
        suv = sb("suv", [128, 64], F32)
        sdv = sb("sdv", [128, 16], F32)
        cfb = sb("cfb", [128, 128], BF16)
        sfb = sb("sfb", [128, 128], BF16)
        mcb = sb("mcb", [128, 8], F32)
        onesF = sb("onesF", [128, 128], BF16)
        onesS = sb("onesS", [128, RK], BF16)
        P = [es.enter_context(nc.psum_tensor(f"P{i}", [128, 512], F32)) for i in range(8)]
        print('SBUF remaining after alloc:', nc.sbuf_bytes_remaining)
        block = es.enter_context(nc.Block())
        B = Bld(nc, block, sems, es)


        def prolog(g):
            g.memset(onesF[:], 1.0)
            g.memset(onesS[:], 1.0)
            return g.memset(carry[0][:], 0.0)
        mk = B.emit("g", [], prolog, "G")
        B.did_w(mk, "onesF", "onesS", "carA")
        for dst, src_, nm in ((sgv, sg, "sgv"), (suv, su, "suv"), (sdv, sd, "sdv"),
                              (cfb, cfm, "cfb"), (sfb, sfm, "sfb"), (mcb, mc, "mcb")):
            mk = B.emit("g", [], (lambda d_, s_: (lambda g: g.dma_start(out=d_[:], in_=s_)))(dst, src_), "DC")
            B.did_w(mk, nm)
        for nm in ("sgv", "suv", "sdv", "cfb", "sfb", "mcb"):
            B.wr[nm] = ("DC", B.cnt["DC"])
        for c in (0, 1):
            mk = B.emit("g", [],
                        lambda g, c_=c: g.dma_start(
                            out=xnb[c_][:, 0:NT * RK].rearrange("p (t r) -> p t r", t=NT),
                            in_=x[:, c_ * RK:(c_ + 1) * RK].rearrange("(t p) r -> p t r", p=128)),
                        "DX%d" % c)
            B.did_w(mk, *["xn%d_t%d" % (c, t) for t in range(NT)])

        XN = [["xn%d_t%d" % (c, t) for t in range(NT)] for c in (0, 1)]

        def dma_wtile(wsrc, mt, bufs, bname, sem_base, nring=2):
            sl = mt % nring
            nm = bname + str(sl)
            mk = B.emit("g", B.dep_w(nm),
                        lambda g, w_=wsrc, m_=mt, d_=bufs[sl]: g.dma_start(
                            out=d_[:], in_=w_[m_].rearrange("p a b -> p (a b)")),
                        "%s%d" % (sem_base, sl))
            B.did_w(mk, nm)

        def dma_wd_piece(idx):
            ot, grp, sl = idx % 16, idx // 16, idx % 3
            nm = "wdb%d" % sl
            mk = B.emit("g", B.dep_w(nm),
                        lambda g, o_=ot, p_=grp, d_=wdb[sl]: g.dma_start(
                            out=d_[:], in_=wd[o_, :, p_ * 16:(p_ + 1) * 16, :].rearrange("p a b -> p (a b)")),
                        "DD%d" % sl)
            B.did_w(mk, nm)

        def ln_chunk(c):
            xname = XN[c]
            ps, pq = 6, 7
            psn, pqn = "P%d" % ps, "P%d" % pq

            def mm_sum(t, c_=c):
                r = None
                for kt in range(NT):
                    r = t.matmul(P[ps][:], onesF[:], xnb[c_][:, kt * RK:(kt + 1) * RK],
                                 start=(kt == 0), stop=(kt == NT - 1))
                return r
            mk = B.emit("t", B.dep_r("onesF", *xname) + B.dep_w(psn), mm_sum, "T")
            B.did_w(mk, psn); B.did_r(mk, "onesF", *xname)
            for kt in range(NT):
                p = kt % 2
                mk = B.emit("s", B.dep_r(xname[kt]) + B.dep_w("sq%d" % p),
                            lambda s_, k_=kt, p_=p, c_=c: s_.square(
                                sqr[:, p_ * RK:(p_ + 1) * RK],
                                xnb[c_][:, k_ * RK:(k_ + 1) * RK]), "S")
                B.did_w(mk, "sq%d" % p); B.did_r(mk, xname[kt])
                mk2 = B.emit("t", B.dep_r("sq%d" % p, "onesF") + (B.dep_w(pqn) if kt == 0 else []),
                             lambda t, k_=kt, p_=p: t.matmul(P[pq][:], onesF[:], sqr[:, p_ * RK:(p_ + 1) * RK],
                                                             start=(k_ == 0), stop=(k_ == NT - 1)), "T")
                B.did_r(mk2, "sq%d" % p, "onesF")
                if kt == NT - 1:
                    B.did_w(mk2, pqn)
            mk = B.emit("v", B.dep_r(psn) + B.dep_w("mb%d" % c),
                        lambda v, c_=c: v.tensor_scalar_mul(mbuf[c_][:], P[ps][:], 1.0 / D), "V")
            B.did_w(mk, "mb%d" % c); B.did_r(mk, psn)
            mk = B.emit("v", B.dep_r("mb%d" % c) + B.dep_w("rsd%d" % c),
                        lambda v, c_=c: v.tensor_tensor(rsdb[c_][:], mbuf[c_][:], mbuf[c_][:], op=ALU.mult), "V")
            B.did_r(mk, "mb%d" % c)
            mk = B.emit("v", [mk] + B.dep_r(pqn),
                        lambda v, c_=c: v.scalar_tensor_tensor(
                            rsdb[c_][:], P[pq][:], 1.0 / D, rsdb[c_][:],
                            op0=ALU.mult, op1=ALU.subtract), "V")
            B.did_r(mk, pqn)
            mk = B.emit("v", [mk],
                        lambda v, c_=c: v.tensor_scalar_add(rsdb[c_][:], rsdb[c_][:], EPS), "V")
            B.did_w(mk, "rsd%d" % c)
            mk = B.emit("s", B.dep_w("rsd%d" % c),
                        lambda s, c_=c: s.activation(rsdb[c_][:], rsdb[c_][:], AF.Sqrt), "S")
            B.did_w(mk, "rsd%d" % c)
            mk = B.emit("v", B.dep_w("rsd%d" % c),
                        lambda v, c_=c: v.reciprocal(rsdb[c_][:], rsdb[c_][:]), "V")
            B.did_w(mk, "rsd%d" % c)
            ne, nsem = ("v", "V") if c == 0 else ("g", "G")
            for kt in range(NT):
                mk = B.emit(ne, B.dep_w(xname[kt]) + B.dep_r("mb%d" % c),
                            lambda v, k_=kt, c_=c: v.tensor_tensor(
                                xnb[c_][:, k_ * RK:(k_ + 1) * RK], xnb[c_][:, k_ * RK:(k_ + 1) * RK],
                                mbuf[c_][:], op=ALU.subtract), nsem)
                B.did_r(mk, "mb%d" % c)
                mk = B.emit(ne, [mk] + B.dep_r("rsd%d" % c),
                            lambda v, k_=kt, c_=c: v.tensor_tensor(
                                xnb[c_][:, k_ * RK:(k_ + 1) * RK], xnb[c_][:, k_ * RK:(k_ + 1) * RK],
                                rsdb[c_][:], op=ALU.mult), nsem)
                B.did_w(mk, xname[kt]); B.did_r(mk, "rsd%d" % c)

        def kv_mm_unit(mt):
            wsl = mt % 4
            for c in (0, 1):
                pb, pbn = P[3 + c], "P%d" % (3 + c)
                kvn = "kv%d_t%d" % (c, mt)

                def mm(t, p_=pb, w_=wbuf[wsl], c_=c):
                    r = None
                    for kt in range(NT):
                        r = t.matmul(p_[:], w_[:, kt * 128:(kt + 1) * 128],
                                     xnb[c_][:, kt * RK:(kt + 1) * RK],
                                     start=(kt == 0), stop=(kt == NT - 1))
                    return r
                mk = B.emit("t", B.dep_r("wb%d" % wsl, *XN[c]) + B.dep_w(pbn), mm, "T")
                B.did_w(mk, pbn); B.did_r(mk, "wb%d" % wsl, *XN[c])
                mk = B.emit("s", B.dep_r(pbn) + B.dep_w(kvn),
                            lambda s_, m_=mt, p_=pb, c_=c: s_.copy(
                                kvb[c_][:, m_ * RK:(m_ + 1) * RK], p_[:]), "S")
                B.did_r(mk, pbn); B.did_w(mk, kvn)
                jq = 2 * (mt % 2) + c
                mk = B.emit("s", B.dep_r(kvn) + B.dep_w("sq%d" % jq),
                            lambda s_, m_=mt, j_=jq, c_=c: s_.square(
                                sqr[:, j_ * RK:(j_ + 1) * RK],
                                kvb[c_][:, m_ * RK:(m_ + 1) * RK]), "S")
                B.did_w(mk, "sq%d" % jq); B.did_r(mk, kvn)
            if mt + 4 < NT:
                dma_wtile(wkv, mt + 4, wbuf, "wb", "DW", nring=4)

        def kv_norm_unit(mt):
            for c in (0, 1):
                kvn = "kv%d_t%d" % (c, mt)
                j = 2 * (mt % 2) + c
                nb, nbn = (P[5], "P5") if c == 0 else (P[6], "P6")
                mk = B.emit("t", B.dep_r("sq%d" % j, "onesF") + B.dep_w(nbn),
                            lambda t, j_=j, n_=nb: t.matmul(
                                n_[:], onesF[:], sqr[:, j_ * RK:(j_ + 1) * RK],
                                start=True, stop=True), "T")
                B.did_w(mk, nbn); B.did_r(mk, "sq%d" % j, "onesF")
                mk = B.emit("s", B.dep_r(nbn) + B.dep_w("rcp%d" % c),
                            lambda s_, j_=c, n_=nb: s_.activation(
                                rcp[:, j_ * RK:(j_ + 1) * RK], n_[:], AF.Sqrt, scale=C_KV), "S")
                B.did_w(mk, "rcp%d" % c); B.did_r(mk, nbn)
                mk = B.emit("v", B.dep_w("rcp%d" % c),
                            lambda v, j_=c: v.reciprocal(
                                rcp[:, j_ * RK:(j_ + 1) * RK], rcp[:, j_ * RK:(j_ + 1) * RK]), "V")
                B.did_w(mk, "rcp%d" % c)
                mk = B.emit("v", B.dep_w(kvn) + B.dep_r("rcp%d" % c),
                            lambda v, m_=mt, j_=c, c_=c: v.tensor_tensor(
                                kvb[c_][:, m_ * RK:(m_ + 1) * RK], kvb[c_][:, m_ * RK:(m_ + 1) * RK],
                                rcp[:, j_ * RK:(j_ + 1) * RK], op=ALU.mult), "V")
                B.did_w(mk, kvn); B.did_r(mk, "rcp%d" % c)
                mk = B.emit("v", B.dep_r(kvn) + B.dep_w("tot%d" % c),
                            lambda v, m_=mt, c_=c: v.tensor_reduce(
                                tots[c_][:, m_:m_ + 1], kvb[c_][:, m_ * RK:(m_ + 1) * RK],
                                mybir.AxisListType.X, ALU.add), "V")
                B.did_r(mk, kvn)
                if mt == NT - 1:
                    B.did_w(mk, "tot%d" % c)
                else:
                    B.did_w(mk, "totpart%d" % c)

        def carry_exchange():
            mk = B.emit("v", B.dep_r("tot0", "tot1") + B.dep_w("ctot"),
                        lambda v: v.tensor_tensor(ctot[:], tots[0][:], tots[1][:], op=ALU.add), "V")
            B.did_w(mk, "ctot"); B.did_r(mk, "tot0", "tot1")
            mk = B.emit("q", B.dep_r("ctot") + B.dep_w("cc_in"),
                        lambda q: q.dma_start(out=cc_in[:, :], in_=ctot[:]), "DI")
            B.did_r(mk, "ctot"); B.did_w(mk, "cc_in")
            mk = B.emit("g", B.dep_r("cc_in") + B.dep_w("cc_out"),
                        lambda g: g.collective_compute(
                            "AllGather", ALU.bypass,
                            replica_groups=[list(range(8))],
                            ins=[cc_in.ap().opt()], outs=[cc_out.ap().opt()]), "CC")
            B.did_r(mk, "cc_in"); B.did_w(mk, "cc_out")
            mk = B.emit("q", B.dep_r("cc_out") + B.dep_w("gth"),
                        lambda q: q.dma_start(
                            out=gth[:].rearrange("p (r t) -> p r t", r=8),
                            in_=cc_out.ap().rearrange("(r p) t -> p r t", p=128)), "DH")
            B.did_w(mk, "gth"); B.did_r(mk, "cc_out")

        def carry_combine():
            for r in range(8):
                mk = B.emit("v", B.dep_r("gth", "mcb") + B.dep_w("carA"),
                            lambda v, r_=r: v.scalar_tensor_tensor(
                                carry[0][:], gth[:, r_ * 16:(r_ + 1) * 16], mcb[:, r_:r_ + 1],
                                carry[0][:], op0=ALU.mult, op1=ALU.add), "V")
                B.did_w(mk, "carA"); B.did_r(mk, "gth", "mcb")
            mk = B.emit("v", B.dep_r("carA", "tot0") + B.dep_w("car1"),
                        lambda v: v.tensor_tensor(carry[1][:], carry[0][:], tots[0][:], op=ALU.add), "V")
            B.did_w(mk, "car1"); B.did_r(mk, "carA", "tot0")

        def scan_step(i, steps):
            mt, c = steps[i]
            kvn = "kv%d_t%d" % (c, mt)
            cn = "carA" if c == 0 else "car1"
            mk = B.emit("v", B.dep_w(kvn) + B.dep_r(cn, "onesS"),
                        lambda v, m_=mt, c_=c: v.tensor_tensor_scan(
                            kvb[c_][:, m_ * RK:(m_ + 1) * RK], onesS[:],
                            kvb[c_][:, m_ * RK:(m_ + 1) * RK],
                            carry[c_][:, m_:m_ + 1],
                            op0=ALU.mult, op1=ALU.add), "V")
            B.did_w(mk, kvn); B.did_r(mk, cn, "onesS")

        def fg_unit(c, mt, bg, bu):
            i2 = mt % 16
            par = mt % 2
            pg, pgn = P[bg], "P%d" % bg
            pu, pun = P[bu], "P%d" % bu

            def mmg(t, p_=pg, w_=wgb[mt % 2], c_=c):
                r = None
                for kt in range(NT):
                    r = t.matmul(p_[:], w_[:, kt * 128:(kt + 1) * 128],
                                 xnb[c_][:, kt * RK:(kt + 1) * RK],
                                 start=(kt == 0), stop=(kt == NT - 1))
                return r
            mk = B.emit("t", B.dep_r("wgb%d" % (mt % 2), *XN[c]) + B.dep_w(pgn), mmg, "T")
            B.did_w(mk, pgn); B.did_r(mk, "wgb%d" % (mt % 2), *XN[c])
            if mt + 2 < 64:
                dma_wtile(wg, mt + 2, wgb, "wgb", "DG")

            def mmu(t, p_=pu, w_=wub[mt % 2], c_=c):
                r = None
                for kt in range(NT):
                    r = t.matmul(p_[:], w_[:, kt * 128:(kt + 1) * 128],
                                 xnb[c_][:, kt * RK:(kt + 1) * RK],
                                 start=(kt == 0), stop=(kt == NT - 1))
                return r
            mk = B.emit("t", B.dep_r("wub%d" % (mt % 2), *XN[c]) + B.dep_w(pun), mmu, "T")
            B.did_w(mk, pun); B.did_r(mk, "wub%d" % (mt % 2), *XN[c])
            if mt + 2 < 64:
                dma_wtile(wu, mt + 2, wub, "wub", "DU")
            mk = B.emit("s", B.dep_r(pgn, "sgv") + B.dep_w("silg%d" % par),
                        lambda s, m_=mt, p_=pg, g_=silg[par]: s.activation(
                            g_[:], p_[:], AF.Silu, scale=sgv[:, m_:m_ + 1]), "S")
            B.did_r(mk, pgn, "sgv"); B.did_w(mk, "silg%d" % par)
            mk = B.emit("v", B.dep_r(pun, "suv", "silg%d" % par) + B.dep_w("fuse_t%d" % i2),
                        lambda v, m_=mt, i_=i2, p_=pu, g_=silg[par]: v.scalar_tensor_tensor(
                            fuse[:, i_ * RK:(i_ + 1) * RK], p_[:], suv[:, m_:m_ + 1], g_[:],
                            op0=ALU.mult, op1=ALU.mult), "V")
            B.did_r(mk, pun, "suv", "silg%d" % par); B.did_w(mk, "fuse_t%d" % i2)

        def fd_unit(grp, ot, bd):
            idx = grp * 16 + ot
            wsl = idx % 3
            pd, pdn = P[bd], "P%d" % bd
            fdeps = [B.dep_r("fuse_t%d" % i)[0] for i in range(16)]
            deps = B.dep_r("wdb%d" % wsl) + fdeps + B.dep_w(pdn)

            def mmd(t, p_=pd, w_=wdb[wsl]):
                r = None
                for i in range(16):
                    r = t.matmul(p_[:], w_[:, i * 128:(i + 1) * 128],
                                 fuse[:, i * RK:(i + 1) * RK],
                                 start=(i == 0), stop=(i == 15))
                return r
            mk = B.emit("t", deps, mmd, "T")
            B.did_w(mk, pdn)
            B.did_r(mk, "wdb%d" % wsl, *["fuse_t%d" % i for i in range(16)])
            if idx + 3 < 64:
                dma_wd_piece(idx + 3)
            if grp == 0:
                mk = B.emit("s", B.dep_r(pdn) + B.dep_w("acc_t%d" % ot),
                            lambda s, o_=ot, p_=pd: s.copy(acc[:, o_ * RK:(o_ + 1) * RK], p_[:]), "S")
            else:
                mk = B.emit("v", B.dep_r(pdn, "acc_t%d" % ot),
                            lambda v, o_=ot, p_=pd: v.tensor_tensor(
                                acc[:, o_ * RK:(o_ + 1) * RK], acc[:, o_ * RK:(o_ + 1) * RK],
                                p_[:], op=ALU.add), "V")
            B.did_r(mk, pdn); B.did_w(mk, "acc_t%d" % ot)

        def epilogue(c):
            outcol = c * RK
            for ot in range(NT):
                par = ot % 2
                mk = B.emit("q", B.dep_w("ard%d" % par) + B.dep_r("ascr%d_t%d" % (c, ot)),
                            lambda q, o_=ot, c_=c, a_=a_rd[par]: q.dma_start(
                                out=a_[:], in_=ascr[o_ * 128:(o_ + 1) * 128, c_ * RK:(c_ + 1) * RK]), "DB%d" % par)
                B.did_w(mk, "ard%d" % par); B.did_r(mk, "ascr%d_t%d" % (c, ot))
                mk = B.emit("g", B.dep_w("xrd%d" % par),
                            lambda g, o_=ot, x_=x_rd[par]: g.dma_start(
                                out=x_[:], in_=x[o_ * 128:(o_ + 1) * 128, outcol:outcol + RK]),
                            "DR%d" % par)
                B.did_w(mk, "xrd%d" % par)
                mk = B.emit("v", B.dep_r("ard%d" % par, "xrd%d" % par) + B.dep_w("axr%d" % par),
                            lambda v, a_=a_rd[par], x_=x_rd[par], r_=axr[par]: v.tensor_tensor(
                                r_[:], a_[:], x_[:], op=ALU.add), "V")
                B.did_r(mk, "ard%d" % par, "xrd%d" % par); B.did_w(mk, "axr%d" % par)
                mk = B.emit("v", B.dep_r("acc_t%d" % ot, "sdv", "axr%d" % par) + B.dep_w("ost%d" % par),
                            lambda v, o_=ot, r_=axr[par], u_=outst[par]: v.scalar_tensor_tensor(
                                u_[:], acc[:, o_ * RK:(o_ + 1) * RK], sdv[:, o_:o_ + 1], r_[:],
                                op0=ALU.mult, op1=ALU.add), "V")
                B.did_r(mk, "acc_t%d" % ot, "sdv", "axr%d" % par); B.did_w(mk, "ost%d" % par)
                mk = B.emit("q", B.dep_r("ost%d" % par),
                            lambda q, o_=ot, u_=outst[par]: q.dma_start(
                                out=out[o_ * 128:(o_ + 1) * 128, outcol:outcol + RK], in_=u_[:]), "DO")
                B.did_r(mk, "ost%d" % par)

        def q_dft_phase():
            steps = [(mt, c) for mt in range(NT) for c in (0, 1)]
            DELAY = 3

            def qproj_mm(i):
                mt, c = steps[i]
                wsl = mt % 4
                pb, pbn = P[3 + c], "P%d" % (3 + c)
                j = i % 4
                jr = i % 2
                qn = "qr%d" % j

                def mm(t, p_=pb, w_=wbuf[wsl], c_=c):
                    r = None
                    for kt in range(NT):
                        r = t.matmul(p_[:], w_[:, kt * 128:(kt + 1) * 128],
                                     xnb[c_][:, kt * RK:(kt + 1) * RK],
                                     start=(kt == 0), stop=(kt == NT - 1))
                    return r
                mk = B.emit("t", B.dep_r("wb%d" % wsl, *XN[c]) + B.dep_w(pbn), mm, "T")
                B.did_w(mk, pbn); B.did_r(mk, "wb%d" % wsl, *XN[c])
                mk = B.emit("s", B.dep_r(pbn) + B.dep_w(qn),
                            lambda s_, j_=j, p_=pb: s_.copy(qr[:, j_ * RK:(j_ + 1) * RK], p_[:]), "S")
                B.did_r(mk, pbn); B.did_w(mk, qn)
                mk = B.emit("s", B.dep_r(qn) + B.dep_w("sq%d" % jr),
                            lambda s_, j_=j, r_=jr: s_.square(
                                sqr[:, r_ * RK:(r_ + 1) * RK], qr[:, j_ * RK:(j_ + 1) * RK]), "S")
                B.did_w(mk, "sq%d" % jr); B.did_r(mk, qn)

            def qproj_norm(i):
                mt, c = steps[i]
                j = i % 4
                jr = i % 2
                qn = "qr%d" % j
                mk = B.emit("t", B.dep_r("sq%d" % jr, "onesF") + B.dep_w("P5"),
                            lambda t, r_=jr: t.matmul(
                                P[5][:], onesF[:], sqr[:, r_ * RK:(r_ + 1) * RK],
                                start=True, stop=True), "T")
                B.did_w(mk, "P5"); B.did_r(mk, "sq%d" % jr, "onesF")
                mk = B.emit("s", B.dep_r("P5") + B.dep_w("rcp%d" % jr),
                            lambda s_, r_=jr: s_.activation(
                                rcp[:, r_ * RK:(r_ + 1) * RK], P[5][:], AF.Sqrt, scale=C_Q), "S")
                B.did_w(mk, "rcp%d" % jr); B.did_r(mk, "P5")
                mk = B.emit("v", B.dep_w("rcp%d" % jr),
                            lambda v, r_=jr: v.reciprocal(
                                rcp[:, r_ * RK:(r_ + 1) * RK], rcp[:, r_ * RK:(r_ + 1) * RK]), "V")
                B.did_w(mk, "rcp%d" % jr)
                mk = B.emit("v", B.dep_w(qn) + B.dep_r("rcp%d" % jr),
                            lambda v, j_=j, r_=jr: v.tensor_tensor(
                                qr[:, j_ * RK:(j_ + 1) * RK], qr[:, j_ * RK:(j_ + 1) * RK],
                                rcp[:, r_ * RK:(r_ + 1) * RK], op=ALU.mult), "V")
                B.did_w(mk, qn); B.did_r(mk, "rcp%d" % jr)

            def dft(i):
                mt, c = steps[i]
                pb, pbn = P[3 + c], "P%d" % (3 + c)
                j = i % 4
                hp = i % 2
                qn = "qr%d" % j
                kvn = "kv%d_t%d" % (c, mt)
                names = [f"hb{hp}_{k}" for k in range(6)]
                qsl = qr[:, j * RK:(j + 1) * RK]
                ksl = kvb[c][:, mt * RK:(mt + 1) * RK]
                for (src_, srcn, mat, matn, dsti, pp, pn) in (
                        (qsl, qn, cfb, "cfb", 0, P[6], "P6"),
                        (qsl, qn, sfb, "sfb", 1, P[7], "P7"),
                        (ksl, kvn, cfb, "cfb", 2, pb, pbn),
                        (ksl, kvn, sfb, "sfb", 3, P[5], "P5")):
                    mk = B.emit("t", B.dep_r(srcn, matn) + B.dep_w(pn),
                                lambda t, s_=src_, w_=mat, p_=pp: t.matmul(
                                    p_[:], w_[:], s_, start=True, stop=True), "T")
                    B.did_w(mk, pn); B.did_r(mk, srcn, matn)
                    if dsti == 0:
                        mk = B.emit("v", B.dep_r(pn) + B.dep_w(names[dsti]),
                                    lambda v, b_=hb[hp][dsti], p_=pp: v.tensor_copy(b_[:], p_[:]), "V")
                    elif dsti == 2:
                        mk = B.emit("v", B.dep_r(pn) + B.dep_w(names[dsti]),
                                    lambda v, b_=hb[hp][dsti], p_=pp: v.tensor_copy(b_[:], p_[:]), "V")
                    else:
                        mk = B.emit("s", B.dep_r(pn) + B.dep_w(names[dsti]),
                                    lambda s_, b_=hb[hp][dsti], p_=pp: s_.copy(b_[:], p_[:]), "S")
                    B.did_r(mk, pn); B.did_w(mk, names[dsti])

                def prods(v, b_=hb[hp]):
                    QC, QS, KC, KS, T5, T6 = b_
                    v.tensor_tensor(T5[:], QC[:], KC[:], op=ALU.mult)
                    v.tensor_tensor(T6[:], QS[:], KS[:], op=ALU.mult)
                    v.tensor_tensor(T5[:], T5[:], T6[:], op=ALU.subtract)
                    v.tensor_tensor(T6[:], QC[:], KS[:], op=ALU.mult)
                    v.tensor_tensor(QC[:], QS[:], KC[:], op=ALU.mult)
                    return v.tensor_tensor(T6[:], T6[:], QC[:], op=ALU.add)
                mk = B.emit("v", B.dep_r(*names) + B.dep_w(names[0], names[4], names[5]), prods, "V")
                B.did_r(mk, *names); B.did_w(mk, names[0], names[4], names[5])

            def dft_inv(i):
                mt, c = steps[i]
                hp = i % 2
                kvn = "kv%d_t%d" % (c, mt)
                names = [f"hb{hp}_{k}" for k in range(6)]
                mk = B.emit("t", B.dep_r(names[4], names[5], "cfb", "sfb") + B.dep_w("P6"),
                            lambda t, b_=hb[hp]: (
                                t.matmul(P[6][:], cfb[:], b_[4][:], start=True, stop=False),
                                t.matmul(P[6][:], sfb[:], b_[5][:], start=False, stop=True))[-1], "T")
                B.did_w(mk, "P6"); B.did_r(mk, names[4], names[5], "cfb", "sfb")
                vhn = "vh%d_t%d" % (c, mt)
                mk = B.emit("s", B.dep_r("P6") + B.dep_w(kvn, vhn),
                            lambda s_, m_=mt, c_=c: s_.copy(
                                kvb[c_][:, m_ * RK:(m_ + 1) * RK], P[6][:]), "S")
                B.did_r(mk, "P6"); B.did_w(mk, kvn); B.did_w(mk, vhn)

            for i in range(len(steps)):
                qproj_mm(i)
                mt, c = steps[i]
                if c == 1 and mt + 4 < NT:
                    dma_wtile(wq, mt + 4, wbuf, "wb", "DW", nring=4)
                if i == 2:
                    carry_combine()
                    for k in range(3):
                        scan_step(k, steps)
                elif i >= 3:
                    scan_step(i, steps)
                if i >= 1:
                    qproj_norm(i - 1)
                if i >= DELAY:
                    dft(i - DELAY)
                if i >= DELAY + 1:
                    dft_inv(i - DELAY - 1)
                if i < 16:
                    fd_unit(0, i, 2)
                else:
                    fg_unit(0, 16 + (i - 16), 0, 1)
            qproj_norm(len(steps) - 1)
            for i in range(len(steps) - DELAY, len(steps)):
                dft(i)
                dft_inv(i - 1)
            dft_inv(len(steps) - 1)

        def wo_unit(o):
            ot, c = o // 2, o % 2
            wsl = ot % 4
            pb, pbn = P[3 + c], "P%d" % (3 + c)
            vh_names = ["vh%d_t%d" % (c, k) for k in range(NT)]

            def mmo(t, p_=pb, w_=wbuf[wsl], c_=c):
                r = None
                for kt in range(NT):
                    r = t.matmul(p_[:], w_[:, kt * 128:(kt + 1) * 128],
                                 kvb[c_][:, kt * RK:(kt + 1) * RK],
                                 start=(kt == 0), stop=(kt == NT - 1))
                return r
            mk = B.emit("t", B.dep_r("wb%d" % wsl, *vh_names) + B.dep_w(pbn), mmo, "T")
            B.did_w(mk, pbn); B.did_r(mk, "wb%d" % wsl, *vh_names)
            an = "ast%d" % c
            mk = B.emit("s", B.dep_r(pbn) + B.dep_w(an),
                        lambda s_, p_=pb, a_=ast[c]: s_.copy(a_[:], p_[:]), "S")
            B.did_r(mk, pbn); B.did_w(mk, an)
            mk = B.emit("q", B.dep_r(an) + B.dep_w("ascr%d_t%d" % (c, ot)),
                        lambda q, o_=ot, c_=c, a_=ast[c]: q.dma_start(
                            out=ascr[o_ * 128:(o_ + 1) * 128, c_ * RK:(c_ + 1) * RK], in_=a_[:]),
                        "DA%d" % c)
            B.did_r(mk, an); B.did_w(mk, "ascr%d_t%d" % (c, ot))
            if c == 1 and ot + 4 < NT:
                dma_wtile(wo, ot + 4, wbuf, "wb", "DW", nring=4)

        ln_chunk(0)
        ln_chunk(1)
        for mt in range(4):
            dma_wtile(wkv, mt, wbuf, "wb", "DW", nring=4)
        dma_wtile(wg, 0, wgb, "wgb", "DG")
        dma_wtile(wu, 0, wub, "wub", "DU")
        dma_wtile(wg, 1, wgb, "wgb", "DG")
        dma_wtile(wu, 1, wub, "wub", "DU")
        for k in range(NT):
            kv_mm_unit(k)
            fg_unit(0, k, 0, 1)
            if k >= 1:
                kv_norm_unit(k - 1)
        kv_norm_unit(NT - 1)
        for mt in range(4):
            dma_wtile(wq, mt, wbuf, "wb", "DW", nring=4)
        dma_wd_piece(0)
        dma_wd_piece(1)
        dma_wd_piece(2)
        carry_exchange()
        q_dft_phase()
        for ot in range(4):
            dma_wtile(wo, ot, wbuf, "wb", "DW", nring=4)
        for o in range(32):
            wo_unit(o)
            if o < 16:
                fd_unit(1, o, 2)
            else:
                fg_unit(0, 32 + (o - 16), 0, 1)
        for c2 in (0, 1):
            for ot2 in range(NT):
                B.wr["ascr%d_t%d" % (c2, ot2)] = ("DA%d" % c2, B.cnt["DA%d" % c2])
        for ot in range(NT):
            fd_unit(2, ot, 4 + ot % 2)
        for j2 in range(NT):
            fg_unit(0, 48 + j2, j2 % 2, 2 + j2 % 2)
        for ot in range(NT):
            fd_unit(3, ot, 4 + ot % 2)
        epilogue(0)
        dma_wtile(wg, 0, wgb, "wgb", "DG")
        dma_wtile(wu, 0, wub, "wub", "DU")
        dma_wtile(wg, 1, wgb, "wgb", "DG")
        dma_wtile(wu, 1, wub, "wub", "DU")
        for grp in range(4):
            for i2 in range(16):
                mt = grp * 16 + i2
                fg_unit(1, mt, mt % 2, 2 + mt % 2)
            if grp == 0:
                for k3 in range(3):
                    dma_wd_piece(k3)
            for ot in range(NT):
                fd_unit(grp, ot, 4 + ot % 2)
        epilogue(1)
        B.emit("q", [("DO", B.cnt["DO"])], lambda q: q.nop(), "G")
        B.flush()
    return nc


_NC_CACHE = None


def kernel(L0_attn_x, L0_attn_W_q, L0_attn_W_kv, L0_attn_W_o,
           L0_ffn_w_gate, L0_ffn_w_up, L0_ffn_w_down,
           L0_ffn_s_gate, L0_ffn_s_up, L0_ffn_s_down):
    x = np.asarray(L0_attn_x, dtype=np.float32)
    Wq = np.asarray(L0_attn_W_q, dtype=np.float32)
    Wkv = np.asarray(L0_attn_W_kv, dtype=np.float32)
    Wo = np.asarray(L0_attn_W_o, dtype=np.float32)
    wg = np.asarray(L0_ffn_w_gate, dtype=np.int8)
    wu = np.asarray(L0_ffn_w_up, dtype=np.int8)
    wd = np.asarray(L0_ffn_w_down, dtype=np.int8)
    sg = np.asarray(L0_ffn_s_gate, dtype=np.float32)
    su = np.asarray(L0_ffn_s_up, dtype=np.float32)
    sdn = np.asarray(L0_ffn_s_down, dtype=np.float32)

    t4a = lambda W: np.ascontiguousarray(W.reshape(16, 128, 16, 128).transpose(0, 3, 2, 1))
    wq_t, wkv_t, wo_t = t4a(Wq), t4a(Wkv), t4a(Wo)
    wg_t = np.ascontiguousarray(wg.reshape(16, 128, 64, 128).transpose(2, 1, 0, 3))
    wu_t = np.ascontiguousarray(wu.reshape(16, 128, 64, 128).transpose(2, 1, 0, 3))
    wd_t = np.ascontiguousarray(wd.reshape(64, 128, 16, 128).transpose(2, 1, 0, 3))
    sg_t = np.ascontiguousarray(sg.reshape(64, 128).T)
    su_t = np.ascontiguousarray(su.reshape(64, 128).T)
    sd_t = np.ascontiguousarray(sdn.reshape(16, 128).T)
    n = np.arange(128)
    ang = 2.0 * np.pi * np.outer(n, n) / 128.0
    cf = np.cos(ang).astype(np.float32)
    sf = np.sin(ang).astype(np.float32)

    in_maps = []
    for c in range(8):
        b, j = c // 4, c % 4
        x_c = np.ascontiguousarray(x[b, j * 1024:(j + 1) * 1024, :].T)
        mrow = np.zeros((8,), dtype=np.float32)
        for r in range(8):
            if r // 4 == b and r % 4 < j:
                mrow[r] = 1.0
        mcb = np.ascontiguousarray(np.tile(mrow, (128, 1)))
        in_maps.append({
            "x": x_c, "wq": wq_t, "wkv": wkv_t, "wo": wo_t,
            "wg": wg_t, "wu": wu_t, "wd": wd_t,
            "sg": sg_t, "su": su_t, "sd": sd_t,
            "cf": cf, "sf": sf, "mc": mcb,
        })

    global _NC_CACHE
    if _NC_CACHE is None:
        _NC_CACHE = build_program()
    res = run_bass_kernel_spmd(_NC_CACHE, in_maps, core_ids=list(range(8)))
    outp = np.empty((2, 4096, D), dtype=np.float32)
    for c in range(8):
        b, j = c // 4, c % 4
        outp[b, j * 1024:(j + 1) * 1024, :] = np.asarray(res.results[c]["out"]).T
    return outp
